# version 17
# baseline (speedup 1.0000x reference)
import contextlib
import numpy as np
import concourse.bass as bass
import concourse.mybir as mybir
from concourse.bass_core_rust import IndirectOffsetOnAxis
from concourse.bass_utils import run_bass_kernel_spmd

F32 = mybir.dt.float32
BF16 = mybir.dt.bfloat16
I32 = mybir.dt.int32
AF = mybir.ActivationFunctionType
ALU = mybir.AluOpType
AX = mybir.AxisListType.X

NCORES = 8
H = 2048
HC = 1024
KT = H // 128
JC = HC // 128
TOK = 2048
HTOK = 1024
NTT = HTOK // 128
F = 1408
FKT = F // 128
FCS = [(0, 384), (384, 384), (768, 384), (1152, 256)]
HCS = [(i * 256, 256) for i in range(8)]
WIN = 512
NWIN = TOK // WIN
QSCALE = 0.125
EPS = 1e-5

ENGS = ("tensor", "vector", "scalar", "gpsimd", "sync")


class Mega:

    def __init__(self, nc):
        self.nc = nc
        self.cnt = {}
        self.lastw = {e: {} for e in ENGS}
        self.semnames = []
        self.ops = None
        self.sems = None

    def sem(self, name):
        if name not in self.cnt:
            self.cnt[name] = 0
            self.semnames.append(name)
        return name

    def begin(self):
        assert self.ops is None
        self.ops = {e: [] for e in ENGS}

    def op(self, eng, build, after=(), inc=None, dma=False):
        waits = []
        for ev in after:
            if ev is None:
                continue
            s, v = ev
            if self.lastw[eng].get(s, 0) >= v:
                continue
            waits.append((s, v))
            self.lastw[eng][s] = v
        ev_out = None
        if inc is not None:
            self.cnt[inc] += 16 if dma else 1
            ev_out = (inc, self.cnt[inc])
        self.ops[eng].append((waits, build, inc, dma))
        return ev_out

    def end(self):
        nc = self.nc
        ops = self.ops
        sems = self.sems
        self.ops = None
        with nc.Block() as block:
            def mk(eng_name):
                def body(engobj):
                    for waits, build, inc, dma in ops[eng_name]:
                        for s, v in waits:
                            engobj.wait_ge(sems[s], v)
                        inst = build(engobj) if build is not None else None
                        if inc is not None:
                            inst.then_inc(sems[inc], 16 if dma else 1)
                return body
            for e in ENGS:
                if ops[e]:
                    getattr(block, e)(mk(e))


def emit_ln_tile(M, sems, src_ap, tt, si, stats, dump, ceps, h_tm, hT, hT_tokstride,
                 eyebf, TB, trk, ev_src, ev_eyebf, ev_ceps):
    t, v, sc = sems
    sb = si * 8
    e_sq = M.op("scalar", lambda s, r=src_ap, sb=sb:
                s.activation(dump[:], r, AF.Square, accum_out=stats[:, sb + 0:sb + 1]),
                after=[ev_src, trk["dump_free"], trk["st_free"][si]], inc=sc)
    trk["dump_free"] = e_sq
    e_sum = M.op("vector", lambda v_, r=src_ap, sb=sb:
                 v_.reduce_sum(stats[:, sb + 2:sb + 3], r, axis=AX),
                 after=[ev_src, trk["st_free"][si]], inc=v)
    e_m = M.op("vector", lambda v_, sb=sb:
               v_.tensor_scalar(stats[:, sb + 3:sb + 4], stats[:, sb + 2:sb + 3],
                                1.0 / H, None, ALU.mult), after=[e_sum], inc=v)
    e_mm2 = M.op("vector", lambda v_, sb=sb:
                 v_.tensor_scalar(stats[:, sb + 5:sb + 6], stats[:, sb + 3:sb + 4],
                                  stats[:, sb + 3:sb + 4], -1.0, ALU.mult, ALU.mult),
                 after=[e_m], inc=v)
    e_var = M.op("vector", lambda v_, sb=sb:
                 v_.scalar_tensor_tensor(stats[:, sb + 6:sb + 7], stats[:, sb + 0:sb + 1],
                                         1.0 / H, stats[:, sb + 5:sb + 6], ALU.mult, ALU.add),
                 after=[e_sq, e_mm2], inc=v)
    e_ln = M.op("scalar", lambda s, sb=sb:
                s.activation(stats[:, sb + 1:sb + 2], stats[:, sb + 6:sb + 7], AF.Ln,
                             bias=ceps[:, 0:1], scale=1.0),
                after=[e_var, ev_ceps], inc=sc)
    e_rstd = M.op("scalar", lambda s, sb=sb:
                  s.activation(stats[:, sb + 7:sb + 8], stats[:, sb + 1:sb + 2], AF.Exp,
                               bias=0.0, scale=-0.5), after=[e_ln], inc=sc)
    e_nmr = M.op("vector", lambda v_, sb=sb:
                 v_.tensor_scalar(stats[:, sb + 4:sb + 5], stats[:, sb + 3:sb + 4],
                                  stats[:, sb + 7:sb + 8], -1.0, ALU.mult, ALU.mult),
                 after=[e_rstd], inc=v)
    e_h = M.op("scalar", lambda s, r=src_ap, sb=sb, si=si:
               s.activation(h_tm[:, si * H:(si + 1) * H], r, AF.Identity,
                            bias=stats[:, sb + 4:sb + 5], scale=stats[:, sb + 7:sb + 8]),
               after=[e_nmr, trk["htm_free"][si]], inc=sc)
    trk["st_free"][si] = e_h
    evs = []
    tev = None
    hT3 = hT[:].rearrange("p (k n) -> p k n", k=KT)
    for k in range(KT):
        bi = (k // 4) % 2
        b = TB[bi]
        tev = M.op("tensor", lambda t_, k=k, si=si, b=b:
                   t_.matmul(b[:, (k % 4) * 128:((k % 4) + 1) * 128],
                             h_tm[:, si * H + k * 128: si * H + (k + 1) * 128],
                             eyebf[:], start=True, stop=True),
                   after=[e_h, ev_eyebf, trk["tb_free"][bi]] if k % 4 == 0 else (),
                   inc=t if (k % 4 == 3) else None)
        if k % 4 == 3:
            kk0 = k - 3
            eev = M.op("vector", lambda v_, kk0=kk0, tt=tt, b=b:
                       v_.tensor_copy(hT3[:, kk0:kk0 + 4, tt * 128:(tt + 1) * 128],
                                      b[:].rearrange("p (k n) -> p k n", k=4)),
                       after=[tev, trk["hT_war"]], inc=v)
            trk["tb_free"][bi] = eev
            evs.append(eev)
    trk["htm_free"][si] = tev
    return evs


CAP = 384
CCT = CAP // 128
SGRP = [(0, 512), (512, 512)]
FCS256 = [(0, 384), (384, 256), (640, 256), (896, 256), (1152, 256)]
HPAIRS = [(i * 512, 512) for i in range(4)]


def ffn_sublayer(nc, M, pfx, SRC, DST, EXP, GWT, EYE, SH):
    t, v, sc = M.sem(pfx + "t"), M.sem(pfx + "v"), M.sem(pfx + "sc")
    gp = M.sem(pfx + "gp")
    qm, qx, qo = M.sem(pfx + "qm"), M.sem(pfx + "qx"), M.sem(pfx + "qo")
    qu = [M.sem(pfx + "qu0"), M.sem(pfx + "qu1")]
    qd = [M.sem(pfx + "qd0"), M.sem(pfx + "qd1")]
    qs = [M.sem(pfx + "qs0"), M.sem(pfx + "qs1")]

    qh = [M.sem(pfx + "qh0"), M.sem(pfx + "qh1")]
    qg = [M.sem(pfx + "qg0"), M.sem(pfx + "qg1")]

    with contextlib.ExitStack() as st:
        routed = st.enter_context(nc.sbuf_tensor(pfx + "routed", [128, NTT * H], F32))
        h_tm = st.enter_context(nc.sbuf_tensor(pfx + "h_tm", [128, 2 * H], BF16))
        hGT = st.enter_context(nc.sbuf_tensor(pfx + "hGT", [128, KT * 512 + KT * 384], BF16, is_transpose_dest=True))
        act_T = st.enter_context(nc.sbuf_tensor(pfx + "act_T", [128, FKT * 512], BF16))
        wub = st.enter_context(nc.sbuf_tensor(pfx + "wub", [128, KT * 384 + KT * 256], BF16))
        wgb = st.enter_context(nc.sbuf_tensor(pfx + "wgb", [128, KT * 384 + KT * 256], BF16))
        wdb = st.enter_context(nc.sbuf_tensor(pfx + "wdb", [128, 2 * FKT * 256], BF16))
        gw_b = st.enter_context(nc.sbuf_tensor(pfx + "gw_b", [128, KT * 8], BF16))
        eyebf = st.enter_context(nc.sbuf_tensor(pfx + "eyebf", [128, 128], BF16))
        eye32 = st.enter_context(nc.sbuf_tensor(pfx + "eye32", [128, 128], F32))
        stats = st.enter_context(nc.sbuf_tensor(pfx + "stats", [128, 16], F32))
        ceps = st.enter_context(nc.sbuf_tensor(pfx + "ceps", [128, 1], F32))
        cm1 = st.enter_context(nc.sbuf_tensor(pfx + "cm1", [128, 1], F32))
        scr = st.enter_context(nc.sbuf_tensor(pfx + "scr", [128, 64], F32))
        wmat = st.enter_context(nc.sbuf_tensor(pfx + "wmat", [128, NTT * 8], F32))
        wmat_bf = st.enter_context(nc.sbuf_tensor(pfx + "wmatb", [128, NTT * 8], BF16))
        Pm1 = st.enter_context(nc.sbuf_tensor(pfx + "Pm1", [128, NTT * 8], F32))
        selm = st.enter_context(nc.sbuf_tensor(pfx + "selm", [128, NTT * 8], F32))
        CJ = st.enter_context(nc.sbuf_tensor(pfx + "CJ", [128, CAP], F32))
        grid = st.enter_context(nc.sbuf_tensor(pfx + "grid", [128, NTT * 2], BF16))
        Sel = st.enter_context(nc.sbuf_tensor(pfx + "Sel", [128, 2 * CAP], BF16))
        SelT = st.enter_context(nc.sbuf_tensor(pfx + "SelT", [128, 2 * CCT * HTOK], BF16))
        logT = SelT[:, CCT * HTOK: CCT * HTOK + 2 * HTOK].bitcast(F32)[0:8, :]
        wgi = st.enter_context(nc.sbuf_tensor(pfx + "wgi", [128, 16], F32))
        wG = st.enter_context(nc.sbuf_tensor(pfx + "wG", [128, 2 * CCT], F32))
        idxf = st.enter_context(nc.sbuf_tensor(pfx + "idxf", [128, 2 * CCT], F32))
        idxi = st.enter_context(nc.sbuf_tensor(pfx + "idxi", [128, 2 * CCT], I32))
        hG = st.enter_context(nc.sbuf_tensor(pfx + "hG", [128, 3 * H], BF16))
        e_sp = st.enter_context(nc.sbuf_tensor(pfx + "e_sp", [128, 2 * CCT * 512], BF16))
        silu_t = st.enter_context(nc.sbuf_tensor(pfx + "silu", [128, 2 * 384], F32))
        act_tm = st.enter_context(nc.sbuf_tensor(pfx + "actm", [128, 2 * 384], BF16))
        psU0 = st.enter_context(nc.psum_tensor([128, 512], F32))
        psU1 = st.enter_context(nc.psum_tensor([128, 512], F32))
        psG0 = st.enter_context(nc.psum_tensor([128, 512], F32))
        psG1 = st.enter_context(nc.psum_tensor([128, 512], F32))
        psT0 = st.enter_context(nc.psum_tensor([128, 512], F32))
        psT1 = st.enter_context(nc.psum_tensor([128, 512], F32))
        psE0 = st.enter_context(nc.psum_tensor([128, 512], F32))
        psE1 = st.enter_context(nc.psum_tensor([128, 512], F32))
        UB, GB, TB, EB = [psU0, psU1], [psG0, psG1], [psT0, psT1], [psE0, psE1]

        M.begin()
        M.op("gpsimd", lambda g: g.dma_start(eyebf[:], EYE), inc=qm, dma=True)
        M.op("sync", lambda s: s.dma_start(eye32[:], EYE), inc=qm, dma=True)
        M.op("gpsimd", lambda g: g.dma_start(
            gw_b[:].rearrange("p (k e) -> p k e", k=KT),
            GWT.rearrange("(k p) e -> p k e", p=128)), inc=qm, dma=True)
        ev_misc = (qm, M.cnt[qm])
        ev_eyebf = ev_misc
        cast_ev = {}
        if pfx == "f1":
            for ci_, (dstw, srcw) in enumerate(EXP["casts"]):
                ev = M.op("gpsimd", lambda g, d=dstw, s_=srcw: g.dma_start(d, s_),
                          inc="f1qc%d" % ci_, dma=True)
                cast_ev[ci_] = ev
            M.cast_evs = dict(cast_ev)
        else:
            cast_ev = M.cast_evs
        ev_ceps = M.op("vector", lambda v_: v_.memset(ceps[:], EPS), inc=v)
        ev_cm1 = M.op("vector", lambda v_: v_.memset(cm1[:], -1.0), inc=v)
        ev_cj = M.op("gpsimd", lambda g: g.iota(CJ[:], [[1, CAP]], base=0, channel_multiplier=0,
                                                allow_small_or_imprecise_dtypes=True), inc=gp)
        gevs = []
        for tt in range(NTT):
            gevs.append(M.op("gpsimd", lambda g, tt=tt: g.iota(
                grid[:, 2 * tt:2 * tt + 1], [[1, 1]], base=0, channel_multiplier=1,
                allow_small_or_imprecise_dtypes=True), inc=gp))
            gevs.append(M.op("vector", lambda v_, tt=tt: v_.memset(
                grid[:, 2 * tt + 1:2 * tt + 2], float(tt * 128)), inc=v))
        ev_grid_g = (gp, M.cnt[gp])
        ev_grid_v = (v, M.cnt[v])

        st_free = [None, None]
        htm_free = [None, None]
        tb_free = [None, None]
        ub_free = [None, None]
        gb_free = [None, None]
        eb_free = [None, None]
        uw_slot_free = [None, None]
        dw_slot_free = [None, None]
        silu_free = [None, None]
        actm_free = [None, None]
        hgt_free = [None, None]
        hg_free = [None, None, None]
        esp_free = [None, None]
        act_T_war = [None]
        sel_war = [None]
        selT_war = [None, None]
        lgT_war = [None]
        scr_free = [None, None]
        lg_free = [None, None]
        prev_store = [None]
        stage_readers = [None]
        logT_war = [None]
        wgi_free = [None]

        for ho in range(2):
            tok0 = ho * HTOK
            ev_x = M.op("sync", lambda s, tok0=tok0: s.dma_start(
                routed[:].rearrange("p (n h) -> p n h", n=NTT),
                SRC[tok0:tok0 + HTOK, :].rearrange("(n p) h -> p n h", p=128)),
                after=[prev_store[0]], inc=qx, dma=True)

            stage_ev = {}
            for tt in range(NTT):
                si = tt % 2
                sb = si * 8
                rsl = routed[:, tt * H:(tt + 1) * H]
                e_sq = M.op("scalar", lambda s, r=rsl, sb=sb, si=si:
                            s.activation(h_tm[:, si * H:(si + 1) * H], r, AF.Square,
                                         accum_out=stats[:, sb + 0:sb + 1]),
                            after=[ev_x, htm_free[si], st_free[si]], inc=sc)
                e_sum = M.op("vector", lambda v_, r=rsl, sb=sb:
                             v_.reduce_sum(stats[:, sb + 2:sb + 3], r, axis=AX),
                             after=[ev_x, st_free[si]], inc=v)
                e_m = M.op("vector", lambda v_, sb=sb:
                           v_.tensor_scalar(stats[:, sb + 3:sb + 4], stats[:, sb + 2:sb + 3],
                                            1.0 / H, None, ALU.mult), after=[e_sum], inc=v)
                e_mm2 = M.op("vector", lambda v_, sb=sb:
                             v_.tensor_scalar(stats[:, sb + 5:sb + 6], stats[:, sb + 3:sb + 4],
                                              stats[:, sb + 3:sb + 4], -1.0, ALU.mult, ALU.mult),
                             after=[e_m], inc=v)
                e_var = M.op("vector", lambda v_, sb=sb:
                             v_.scalar_tensor_tensor(stats[:, sb + 6:sb + 7], stats[:, sb + 0:sb + 1],
                                                     1.0 / H, stats[:, sb + 5:sb + 6], ALU.mult, ALU.add),
                             after=[e_sq, e_mm2], inc=v)
                e_ln = M.op("scalar", lambda s, sb=sb:
                            s.activation(stats[:, sb + 1:sb + 2], stats[:, sb + 6:sb + 7], AF.Ln,
                                         bias=ceps[:, 0:1], scale=1.0), after=[e_var, ev_ceps], inc=sc)
                e_rstd = M.op("scalar", lambda s, sb=sb:
                              s.activation(stats[:, sb + 7:sb + 8], stats[:, sb + 1:sb + 2], AF.Exp,
                                           bias=0.0, scale=-0.5), after=[e_ln], inc=sc)
                e_nmr = M.op("vector", lambda v_, sb=sb:
                             v_.tensor_scalar(stats[:, sb + 4:sb + 5], stats[:, sb + 3:sb + 4],
                                              stats[:, sb + 7:sb + 8], -1.0, ALU.mult, ALU.mult),
                             after=[e_rstd], inc=v)
                e_h = M.op("scalar", lambda s, r=rsl, sb=sb, si=si:
                           s.activation(h_tm[:, si * H:(si + 1) * H], r, AF.Identity,
                                        bias=stats[:, sb + 4:sb + 5], scale=stats[:, sb + 7:sb + 8]),
                           after=[e_nmr], inc=sc)
                st_free[si] = e_h
                e_st = M.op("sync", lambda s, tt=tt, si=si: s.dma_start(
                    SH[tt * 128:(tt + 1) * 128, :], h_tm[:, si * H:(si + 1) * H]),
                    after=[e_h, stage_readers[0]], inc=qs[si], dma=True)
                htm_free[si] = e_st
                stage_ev[tt] = e_st
            all_staged = [stage_ev[tt] for tt in range(NTT)]

            def hgt_base_ks(slot):
                return (0, 512) if slot == 0 else (KT * 512, 384)

            def build_group(g0, width, slot):
                assert slot == 0 and width <= 512
                deps = [stage_ev[tt2] for tt2 in range(g0 // 128, (g0 + width) // 128)]
                deps += [hgt_free[slot]]
                evs = []
                for k in range(KT):
                    evs.append(M.op("sync", lambda s, k=k, g0=g0, width=width: s.dma_start(
                        hGT[:, k * 512: k * 512 + width],
                        SH[g0:g0 + width, k * 128:(k + 1) * 128], transpose=True),
                        after=deps if k == 0 else (), inc=qh[slot], dma=True))
                return evs[-1]

            def gather_issue(islot, ev_idx):
                evs = []
                for ct in range(CCT):
                    e_g = M.op("gpsimd", lambda g, ct=ct, islot=islot: g.indirect_dma_start(
                        hG[:, ct * H:(ct + 1) * H], None, SH,
                        IndirectOffsetOnAxis(ap=idxi[:, islot * CCT + ct: islot * CCT + ct + 1], axis=0)),
                        after=(all_staged + [ev_idx, hg_free[ct]]), inc=qg[ct % 2], dma=True)
                    evs.append(e_g)
                return evs

            def gather_transposes(eslot, g_evs):
                last_ev = None
                for ct in range(CCT):
                    for k4 in range(4):
                        b = TB[k4 % 2]
                        tev = None
                        for j in range(4):
                            k = k4 * 4 + j
                            tev = M.op("tensor", lambda t_, k=k, j=j, b=b, ct=ct:
                                       t_.matmul(b[:, j * 128:(j + 1) * 128],
                                                 hG[:, ct * H + k * 128: ct * H + (k + 1) * 128],
                                                 eyebf[:], start=True, stop=True),
                                       after=[g_evs[ct], ev_eyebf, tb_free[k4 % 2], hgt_free[eslot]] if j == 0 else (),
                                       inc=t if j == 3 else None)
                        base, ks = hgt_base_ks(eslot)
                        hgt3 = hGT[:, base:base + KT * ks].rearrange("p (k n) -> p k n", k=KT)
                        eev = M.op("scalar", lambda s_, k4=k4, ct=ct, b=b, hgt3=hgt3:
                                   s_.copy(hgt3[:, k4 * 4: k4 * 4 + 4, ct * 128:(ct + 1) * 128],
                                           b[:].rearrange("p (k n) -> p k n", k=4)),
                                   after=[tev], inc=sc)
                        tb_free[k4 % 2] = eev
                        last_ev = eev
                    hg_free[ct] = tev
                return last_ev

            grp_ev = {}
            ev_lgT = {}
            ug_loads = {}
            d_loads = {}
            routed_last = {}

            def sh_w(mat, fh):
                def srcf(f0, fw, mat=mat, fh=fh):
                    return mat[:, fh * F + f0: fh * F + f0 + fw].rearrange("(k p) f -> p k f", p=128)
                return srcf

            def sh_d(mat, fh):
                def srcf(h0, hw, mat=mat, fh=fh):
                    return mat[fh * F:(fh + 1) * F, h0:h0 + hw].rearrange("(k p) h -> p k h", p=128)
                return srcf

            SHU_ap, SHG_ap, SHD_ap = EXP["shared"]
            def unit(hgt_slot, width, wcol_of, store_tile0, scatter, wu_src, wg_src, wd_src,
                     hgt_ready, mid_cb=None, cast_dep=None):
                nct = width // 128

                def issue_ug_load(ci):
                    f0, fw = FCS256[ci]
                    slot = ci % 2
                    sbase = 0 if slot == 0 else KT * 384
                    wsl_u = wub[:, sbase: sbase + KT * fw]
                    wsl_g = wgb[:, sbase: sbase + KT * fw]
                    dep = [uw_slot_free[slot], cast_dep]
                    M.op("sync", lambda g, w=wsl_u, f0=f0, fw=fw: g.dma_start(
                        w.rearrange("p (k f) -> p k f", k=KT), wu_src(f0, fw)),
                        after=dep, inc=qu[slot], dma=True)
                    e_g2 = M.op("sync", lambda g, w=wsl_g, f0=f0, fw=fw: g.dma_start(
                        w.rearrange("p (k f) -> p k f", k=KT), wg_src(f0, fw)),
                        after=dep, inc=qu[slot], dma=True)
                    ug_loads[ci] = (e_g2, wsl_u, wsl_g, slot)

                issue_ug_load(0)
                issue_ug_load(1)

                def issue_d_load(hc2):
                    h0 = hc2 * 256
                    slot = hc2 % 2
                    wsl = wdb[:, slot * FKT * 256:(slot + 1) * FKT * 256]
                    e_d = M.op("sync", lambda g, w=wsl, h0=h0: g.dma_start(
                        w.rearrange("p (k h) -> p k h", k=FKT), wd_src(h0, 256)),
                        after=[dw_slot_free[slot], cast_dep], inc=qd[slot], dma=True)
                    d_loads[hc2] = (e_d, wsl, slot)

                issue_d_load(0)
                issue_d_load(1)
                mid_ret = mid_cb() if mid_cb is not None else None
                pend = [None]

                def emit_transposes(p):
                    e_act, si2, fw, f0, ct = p
                    nblk = fw // 128
                    tb_i = ct % 2
                    tev = None
                    for j in range(nblk):
                        tev = M.op("tensor", lambda t_, j=j, i=si2, b=TB[tb_i]:
                                   t_.matmul(b[:, j * 128:(j + 1) * 128],
                                             act_tm[:, i * 384 + j * 128: i * 384 + (j + 1) * 128],
                                             eyebf[:], start=True, stop=True),
                                   after=[e_act, ev_eyebf, tb_free[tb_i], act_T_war[0]] if j == 0 else (),
                                   inc=t if j == nblk - 1 else None)
                    fk0 = f0 // 128
                    aT3 = act_T[:].rearrange("p (k n) -> p k n", k=FKT)
                    eev = M.op("scalar", lambda s_, fk0=fk0, nblk=nblk, ct=ct, b=TB[tb_i]:
                               s_.copy(aT3[:, fk0:fk0 + nblk, ct * 128:(ct + 1) * 128],
                                       b[:, 0:nblk * 128].rearrange("p (k n) -> p k n", k=nblk)),
                               after=[tev], inc=sc)
                    tb_free[tb_i] = eev
                    actm_free[si2] = tev
                    return eev

                for ci, (f0, fw) in enumerate(FCS256):
                    e_ug, wsl_u, wsl_g, slot = ug_loads[ci]
                    last_mm_slot = None
                    for ct in range(nct):
                        bu, bg = UB[ct % 2], GB[ct % 2]
                        deps = [hgt_ready, e_ug, ub_free[ct % 2], gb_free[ct % 2]]
                        mm = None
                        base, ks = hgt_base_ks(hgt_slot)
                        for k in range(KT):
                            lh = hGT[:, base + k * ks + ct * 128: base + k * ks + (ct + 1) * 128]
                            M.op("tensor", lambda t_, bu=bu, lh=lh, w=wsl_u, k=k, fw=fw:
                                 t_.matmul(bu[:, 0:fw], lh, w[:, k * fw:(k + 1) * fw],
                                           start=(k == 0), stop=(k == KT - 1)),
                                 after=deps if k == 0 else ())
                            mm = M.op("tensor", lambda t_, bg=bg, lh=lh, w=wsl_g, k=k, fw=fw:
                                      t_.matmul(bg[:, 0:fw], lh, w[:, k * fw:(k + 1) * fw],
                                                start=(k == 0), stop=(k == KT - 1)),
                                      inc=t if k == KT - 1 else None)
                        last_mm_slot = mm
                        si2 = ct % 2
                        e_silu = M.op("scalar", lambda s, bg=bg, fw=fw, i=si2:
                                      s.activation(silu_t[:, i * 384: i * 384 + fw], bg[:, 0:fw], AF.Silu),
                                      after=[mm, silu_free[si2]], inc=sc)
                        gb_free[ct % 2] = e_silu
                        wc_ap = wcol_of(ct)
                        if wc_ap is None:
                            e_act = M.op("vector", lambda v_, bu=bu, fw=fw, i=si2:
                                         v_.scalar_tensor_tensor(act_tm[:, i * 384:i * 384 + fw],
                                                                 bu[:, 0:fw], 1.0,
                                                                 silu_t[:, i * 384:i * 384 + fw],
                                                                 ALU.mult, ALU.mult),
                                         after=[mm, e_silu, actm_free[si2]], inc=v)
                        else:
                            e_act = M.op("vector", lambda v_, bu=bu, fw=fw, i=si2, wc=wc_ap:
                                         v_.scalar_tensor_tensor(act_tm[:, i * 384:i * 384 + fw],
                                                                 bu[:, 0:fw], wc,
                                                                 silu_t[:, i * 384:i * 384 + fw],
                                                                 ALU.mult, ALU.mult),
                                         after=[mm, e_silu, actm_free[si2]], inc=v)
                        ub_free[ct % 2] = e_act
                        silu_free[si2] = e_act
                        if pend[0] is not None:
                            emit_transposes(pend[0])
                        pend[0] = (e_act, si2, fw, f0, ct)
                    uw_slot_free[slot] = last_mm_slot
                    if ci + 2 < len(FCS256):
                        issue_ug_load(ci + 2)
                act_T_ready = emit_transposes(pend[0])
                pend[0] = None

                last_dmm = None
                last_scatter = None
                pend_sc = [None]

                def emit_scatter(ps):
                    hp, espslot, pair_evs = ps
                    nonlocal last_scatter
                    selT_ready, sslot = scatter
                    for tt2 in range(NTT):
                        gbk = GB[tt2 % 2]
                        mm = None
                        for ct in range(nct):
                            mm = M.op("tensor", lambda t_, b=gbk, ct=ct, tt2=tt2, es=espslot, ss=sslot:
                                      t_.matmul(b[:], SelT[:, ss * CCT * HTOK + ct * HTOK + tt2 * 128:
                                                           ss * CCT * HTOK + ct * HTOK + (tt2 + 1) * 128],
                                                e_sp[:, es * CCT * 512 + ct * 512: es * CCT * 512 + (ct + 1) * 512],
                                                start=(ct == 0), stop=(ct == nct - 1)),
                                      after=(pair_evs + [selT_ready, gb_free[tt2 % 2]]) if ct == 0 else (),
                                      inc=t if ct == nct - 1 else None)
                        last_scatter = mm
                        rsl2 = routed[:, tt2 * H + hp * 512: tt2 * H + (hp + 1) * 512]
                        e_r = M.op("vector", lambda v_, b=gbk, r=rsl2:
                                   v_.scalar_tensor_tensor(r, b[:], 0.5, r, ALU.mult, ALU.add),
                                   after=[mm, routed_last.get((tt2, 2 * hp)),
                                          routed_last.get((tt2, 2 * hp + 1))], inc=v)
                        routed_last[(tt2, 2 * hp)] = e_r
                        routed_last[(tt2, 2 * hp + 1)] = e_r
                        gb_free[tt2 % 2] = e_r
                    esp_free[espslot] = last_scatter

                for hp in range(4):
                    pair_evs = []
                    espslot = hp % 2
                    for hs in range(2):
                        hc2 = hp * 2 + hs
                        e_d, wsl, slot = d_loads[hc2]
                        last_mm_slot = None
                        for ct in range(nct):
                            bi = (hc2 * nct + ct) % 2
                            bank = EB[bi]
                            mm = None
                            for fk in range(FKT):
                                la = act_T[:, fk * 512 + ct * 128: fk * 512 + (ct + 1) * 128]
                                mm = M.op("tensor", lambda t_, b=bank, la=la, w=wsl, fk=fk:
                                          t_.matmul(b[:, 0:256], la, w[:, fk * 256:(fk + 1) * 256],
                                                    start=(fk == 0), stop=(fk == FKT - 1)),
                                          after=[act_T_ready, e_d, eb_free[bi]] if fk == 0 else (),
                                          inc=t if fk == FKT - 1 else None)
                            last_mm_slot = mm
                            last_dmm = mm
                            if scatter is None:
                                rsl2 = routed[:, (store_tile0 + ct) * H + hc2 * 256:
                                              (store_tile0 + ct) * H + hc2 * 256 + 256]
                                e_r = M.op("vector", lambda v_, b=bank, r=rsl2:
                                           v_.scalar_tensor_tensor(r, b[:, 0:256], 0.5, r, ALU.mult, ALU.add),
                                           after=[mm, routed_last.get((store_tile0 + ct, hc2))], inc=v)
                                routed_last[(store_tile0 + ct, hc2)] = e_r
                                eb_free[bi] = e_r
                            else:
                                ee = M.op("scalar", lambda s_, b=bank, ct=ct, hs=hs, es=espslot:
                                          s_.copy(
                                              e_sp[:, es * CCT * 512 + ct * 512 + hs * 256:
                                                   es * CCT * 512 + ct * 512 + hs * 256 + 256],
                                              b[:, 0:256]),
                                          after=[mm, esp_free[espslot] if (ct == 0 and hs == 0) else None],
                                          inc=sc)
                                pair_evs.append(ee)
                                eb_free[bi] = ee
                        dw_slot_free[slot] = last_mm_slot
                        if hc2 + 2 < 8:
                            issue_d_load(hc2 + 2)
                    if scatter is not None:
                        if pend_sc[0] is not None:
                            emit_scatter(pend_sc[0])
                        pend_sc[0] = (hp, espslot, pair_evs)
                if scatter is not None and pend_sc[0] is not None:
                    emit_scatter(pend_sc[0])
                act_T_war[0] = last_dmm
                return last_dmm, last_scatter, mid_ret

            for gi, (g0, width) in enumerate(SGRP):
                bev = build_group(g0, width, 0)
                grp_ev[gi] = bev
                mm = None
                for k in range(KT):
                    mm = M.op("tensor", lambda t_, k=k, width=width:
                              t_.matmul(psE0[0:8, 0:width], gw_b[:, k * 8:(k + 1) * 8],
                                        hGT[:, k * 512: k * 512 + width],
                                        start=(k == 0), stop=(k == KT - 1)),
                              after=[bev, ev_misc, eb_free[0]] if k == 0 else (),
                              inc=t if k == KT - 1 else None)
                ev_e = M.op("vector", lambda v_, g0=g0, width=width:
                            v_.tensor_copy(logT[0:8, g0:g0 + width], psE0[0:8, 0:width]),
                            after=[mm, logT_war[0], selT_war[1]], inc=v)
                ev_lgT[gi] = ev_e
                eb_free[0] = ev_e
                for fh in range(2):
                    dmm, _, _ = unit(0, width, lambda ct: None, g0 // 128, None,
                                     sh_w(SHU_ap, fh), sh_w(SHG_ap, fh), sh_d(SHD_ap, fh), bev,
                                     cast_dep=cast_ev[2])
                    hgt_free[0] = dmm

            wmat_ev = {}
            for tt in range(NTT):
                bk = EB[tt % 2]
                m2 = M.op("tensor", lambda t_, tt=tt, b=bk:
                          t_.matmul(b[:, 0:8], logT[0:8, tt * 128:(tt + 1) * 128],
                                    eye32[0:8, 0:8], start=True, stop=True),
                          after=[ev_lgT[0], ev_lgT[1], ev_misc, eb_free[tt % 2], lg_free[tt % 2]], inc=t)
                lgT_war[0] = m2
                li = tt % 2
                sb = li * 32
                e1 = M.op("vector", lambda v_, b=bk, sb=sb:
                          v_.reduce_max(scr[:, sb + 0:sb + 1], b[:, 0:8], axis=AX, negate=True),
                          after=[m2], inc=v)
                e2 = M.op("scalar", lambda s, b=bk, sb=sb:
                          s.activation(scr[:, sb + 8:sb + 16], b[:, 0:8], AF.Exp, bias=scr[:, sb + 0:sb + 1],
                                       accum_out=scr[:, sb + 1:sb + 2]),
                          after=[e1, scr_free[li]], inc=sc)
                eb_free[li] = e2
                lg_free[li] = e2
                e3 = M.op("vector", lambda v_, sb=sb:
                          v_.reciprocal(scr[:, sb + 2:sb + 3], scr[:, sb + 1:sb + 2]), after=[e2], inc=v)
                e4 = M.op("vector", lambda v_, sb=sb:
                          v_.tensor_scalar(scr[:, sb + 8:sb + 16], scr[:, sb + 8:sb + 16],
                                           scr[:, sb + 2:sb + 3], None, ALU.mult), after=[e3], inc=v)
                e5 = M.op("vector", lambda v_, sb=sb:
                          v_.tensor_tensor(scr[:, sb + 16:sb + 20], scr[:, sb + 8:sb + 16:2],
                                           scr[:, sb + 9:sb + 16:2], ALU.max), after=[e4], inc=v)
                e6 = M.op("vector", lambda v_, sb=sb:
                          v_.reduce_max(scr[:, sb + 0:sb + 1], scr[:, sb + 16:sb + 20], axis=AX),
                          after=[e5], inc=v)
                e7 = M.op("vector", lambda v_, sb=sb:
                          v_.tensor_scalar(scr[:, sb + 20:sb + 24], scr[:, sb + 16:sb + 20],
                                           scr[:, sb + 0:sb + 1], None, ALU.is_ge), after=[e6], inc=v)
                e8 = M.op("vector", lambda v_, sb=sb:
                          v_.scalar_tensor_tensor(scr[:, sb + 3:sb + 7], scr[:, sb + 20:sb + 24],
                                                  -1e30, scr[:, sb + 16:sb + 20], ALU.mult, ALU.add),
                          after=[e7], inc=v)
                e9 = M.op("vector", lambda v_, sb=sb:
                          v_.reduce_max(scr[:, sb + 1:sb + 2], scr[:, sb + 3:sb + 7], axis=AX),
                          after=[e8], inc=v)
                e10 = M.op("vector", lambda v_, sb=sb:
                           v_.tensor_scalar(scr[:, sb + 20:sb + 24], scr[:, sb + 16:sb + 20],
                                            scr[:, sb + 1:sb + 2], None, ALU.is_ge), after=[e9], inc=v)
                e12 = M.op("vector", lambda v_, sb=sb:
                           v_.tensor_mul(scr[:, sb + 24:sb + 32:2], scr[:, sb + 8:sb + 16:2],
                                         scr[:, sb + 20:sb + 24]), after=[e10], inc=v)
                e13 = M.op("vector", lambda v_, sb=sb:
                           v_.tensor_mul(scr[:, sb + 25:sb + 32:2], scr[:, sb + 9:sb + 16:2],
                                         scr[:, sb + 20:sb + 24]), after=[e12], inc=v)
                e14 = M.op("vector", lambda v_, sb=sb:
                           v_.reduce_max(scr[:, sb + 0:sb + 1], scr[:, sb + 24:sb + 32], axis=AX),
                           after=[e13], inc=v)
                e15 = M.op("vector", lambda v_, sb=sb:
                           v_.tensor_scalar(scr[:, sb + 8:sb + 16], scr[:, sb + 24:sb + 32],
                                            scr[:, sb + 0:sb + 1], None, ALU.is_ge), after=[e14], inc=v)
                e16 = M.op("vector", lambda v_, sb=sb:
                           v_.scalar_tensor_tensor(scr[:, sb + 16:sb + 24], scr[:, sb + 8:sb + 16],
                                                   -1e30, scr[:, sb + 24:sb + 32], ALU.mult, ALU.add),
                           after=[e15], inc=v)
                e17 = M.op("vector", lambda v_, sb=sb:
                           v_.reduce_max(scr[:, sb + 1:sb + 2], scr[:, sb + 16:sb + 24], axis=AX),
                           after=[e16], inc=v)
                e18 = M.op("vector", lambda v_, sb=sb:
                           v_.tensor_scalar(scr[:, sb + 8:sb + 16], scr[:, sb + 24:sb + 32],
                                            scr[:, sb + 1:sb + 2], None, ALU.is_ge), after=[e17], inc=v)
                e19 = M.op("vector", lambda v_, tt=tt, sb=sb:
                           v_.tensor_mul(wmat[:, tt * 8:(tt + 1) * 8], scr[:, sb + 24:sb + 32],
                                         scr[:, sb + 8:sb + 16]), after=[e18], inc=v)
                wmat_ev[tt] = e19
                scr_free[li] = e19
            all_wmat = [wmat_ev[tt] for tt in range(NTT)]
            ev_wb = M.op("vector", lambda v_: v_.tensor_copy(wmat_bf[:], wmat[:]),
                         after=all_wmat + [sel_war[0]], inc=v)
            ev_sm = M.op("vector", lambda v_: v_.tensor_scalar(selm[:], wmat[:], 0.0, None, ALU.is_gt),
                         after=all_wmat, inc=v)
            wtev = None
            for tt in range(NTT):
                bk = EB[tt % 2]
                mmw = M.op("tensor", lambda t_, tt=tt, b=bk:
                           t_.matmul(b[0:8, 0:128], wmat[:, tt * 8:(tt + 1) * 8], eye32[:],
                                     start=True, stop=True),
                           after=[wmat_ev[tt], eb_free[tt % 2], logT_war[0], (qm, M.cnt[qm])], inc=t)
                wtev = M.op("vector", lambda v_, tt=tt, b=bk:
                            v_.tensor_copy(logT[0:8, tt * 128:(tt + 1) * 128], b[0:8, 0:128]),
                            after=[mmw, lgT_war[0]], inc=v)
                eb_free[tt % 2] = wtev
            STTv = SelT[:].bitcast(F32)[0:8, 0:HTOK]
            ev_st2 = M.op("vector", lambda v_: v_.tensor_scalar(STTv, logT[0:8, :], 0.0, None, ALU.is_gt),
                          after=[wtev, sel_war[0]], inc=v)
            ev_scan = M.op("vector", lambda v_: v_.tensor_tensor_scan(
                logT[0:8, :], STTv, STTv, 0.0, ALU.add, ALU.bypass),
                after=[ev_st2], inc=v)
            pm_evs = []
            for tt in range(NTT):
                bk = EB[tt % 2]
                mmp = M.op("tensor", lambda t_, tt=tt, b=bk:
                           t_.matmul(b[:, 0:8], logT[0:8, tt * 128:(tt + 1) * 128], eye32[0:8, 0:8],
                                     start=True, stop=True),
                           after=[ev_scan, eb_free[tt % 2]], inc=t)
                logT_war[0] = mmp
                pe = M.op("scalar", lambda s, tt=tt, b=bk:
                          s.activation(Pm1[:, tt * 8:(tt + 1) * 8], b[:, 0:8], AF.Identity,
                                       bias=cm1[:, 0:1], scale=1.0),
                          after=[mmp, ev_cm1], inc=sc)
                pm_evs.append(pe)
                eb_free[tt % 2] = pe

            idx_war = [None, None]

            sel_free = [None, None]

            def prep(e):
                islot = e % 2
                bkA = TB[0]
                bkB = TB[1]
                first = [True]
                st_evs = []
                for tt in range(NTT):
                    ss = tt % 2
                    se = M.op("vector", lambda v_, tt=tt, e=e, ss=ss:
                              v_.tensor_scalar(Sel[:, ss * CAP:(ss + 1) * CAP], CJ[:],
                                               Pm1[:, tt * 8 + e: tt * 8 + e + 1],
                                               selm[:, tt * 8 + e: tt * 8 + e + 1],
                                               ALU.is_equal, ALU.mult),
                              after=[pm_evs[tt], ev_sm, ev_cj, sel_free[ss]], inc=v)
                    mm = None
                    for ct in range(CCT):
                        stf = first[0]
                        first[0] = False
                        M.op("tensor", lambda t_, b=bkA, tt=tt, ct=ct, e=e, ss=ss, stf=stf:
                             t_.matmul(b[:, ct * 4:ct * 4 + 1],
                                       Sel[:, ss * CAP + ct * 128: ss * CAP + (ct + 1) * 128],
                                       wmat_bf[:, tt * 8 + e: tt * 8 + e + 1],
                                       start=stf, stop=False),
                             after=([se, ev_wb, tb_free[0]] if stf else [se]))
                        is_last = (tt == NTT - 1 and ct == CCT - 1)
                        mm = M.op("tensor", lambda t_, b=bkA, tt=tt, ct=ct, ss=ss, last=is_last:
                                  t_.matmul(b[:, ct * 4 + 1:ct * 4 + 3],
                                            Sel[:, ss * CAP + ct * 128: ss * CAP + (ct + 1) * 128],
                                            grid[:, 2 * tt:2 * tt + 2],
                                            start=False, stop=last),
                                  after=[ev_grid_g, ev_grid_v] if tt == 0 and ct == 0 else (),
                                  inc=t if is_last else None)
                    tev = None
                    for ct in range(CCT):
                        tev = M.op("tensor", lambda t_, b=bkB, tt=tt, ct=ct, ss=ss:
                                   t_.matmul(b[:, ct * 128:(ct + 1) * 128],
                                             Sel[:, ss * CAP + ct * 128: ss * CAP + (ct + 1) * 128],
                                             eyebf[:], start=True, stop=True),
                                   after=[se, tb_free[1]] if ct == 0 else (),
                                   inc=t if ct == CCT - 1 else None)
                    sel_free[ss] = tev
                    sT3 = SelT[:, islot * CCT * HTOK:(islot + 1) * CCT * HTOK].rearrange(
                        "p (c n) -> p c n", c=CCT)
                    ste = M.op("vector", lambda v_, tt=tt, b=bkB, sT3=sT3:
                               v_.tensor_copy(sT3[:, :, tt * 128:(tt + 1) * 128],
                                              b[:, 0:CCT * 128].rearrange("p (c n) -> p c n", c=CCT)),
                               after=[tev, selT_war[islot] if tt == 0 else None], inc=v)
                    tb_free[1] = ste
                    st_evs.append(ste)
                idx_mm = mm
                ev_ev = M.op("vector", lambda v_: v_.tensor_copy(wgi[:, 0:12], bkA[:, 0:12]),
                             after=[idx_mm, wgi_free[0]], inc=v)
                tb_free[0] = ev_ev
                wi_evs = []
                for ct in range(CCT):
                    e_wg = M.op("vector", lambda v_, ct=ct, islot=islot:
                                v_.tensor_copy(wG[:, islot * CCT + ct: islot * CCT + ct + 1],
                                               wgi[:, ct * 4:ct * 4 + 1]),
                                after=[ev_ev], inc=v)
                    e_ix = M.op("vector", lambda v_, ct=ct, islot=islot:
                                v_.tensor_tensor(idxf[:, islot * CCT + ct: islot * CCT + ct + 1],
                                                 wgi[:, ct * 4 + 1:ct * 4 + 2],
                                                 wgi[:, ct * 4 + 2:ct * 4 + 3], ALU.add),
                                after=[ev_ev], inc=v)
                    wi_evs += [e_wg, e_ix]
                ev_idx = M.op("vector", lambda v_, islot=islot: v_.tensor_copy(
                    idxi[:, islot * CCT:(islot + 1) * CCT], idxf[:, islot * CCT:(islot + 1) * CCT]),
                    after=wi_evs + [idx_war[islot]], inc=v)
                wgi_free[0] = ev_idx
                g_evs = gather_issue(islot, ev_idx)
                idx_war[islot] = g_evs[-1]
                return {"islot": islot, "selT_ready": st_evs[-1], "g_evs": g_evs}

            pr = prep(0)
            hr = None
            for e in range(8):
                islot = pr["islot"]
                hr = gather_transposes(islot, pr["g_evs"])
                nxt = [None]

                def mid(e=e, nxt=nxt):
                    if e + 1 < 8:
                        nxt[0] = prep(e + 1)
                    return None

                dmm, lsc, _ = unit(islot, CAP,
                                   lambda ct, islot=islot: wG[:, islot * CCT + ct: islot * CCT + ct + 1],
                                   None, (pr["selT_ready"], islot),
                                   EXP["routed"][e][0], EXP["routed"][e][1], EXP["routed"][e][2],
                                   hr, mid_cb=mid, cast_dep=cast_ev[3 + e * 3 + 2])
                hgt_free[islot] = dmm
                selT_war[islot] = lsc
                pr = nxt[0]

            stage_readers[0] = hr
            trail = [routed_last[k] for k in routed_last]
            ev_out = M.op("sync", lambda s, tok0=tok0: s.dma_start(
                DST[tok0:tok0 + HTOK, :].rearrange("(n p) h -> p n h", p=128),
                routed[:].rearrange("p (n h) -> p n h", n=NTT)),
                after=trail, inc=qo, dma=True)
            prev_store[0] = ev_out
        M.op("sync", lambda s: s.nop(), after=[prev_store[0]])
        M.end()


def attn_sublayer(nc, M, pfx, SRC, DST, WAP, EYE, BT):
    t, v, sc = M.sem(pfx + "t"), M.sem(pfx + "v"), M.sem(pfx + "sc")
    qm, qo = M.sem(pfx + "qm"), M.sem(pfx + "qo")
    qwc, qwq, qwk, qwv, qwd = (M.sem(pfx + n) for n in ("qwc", "qwq", "qwk", "qwv", "qwd"))
    qx = [M.sem(pfx + "qx0"), M.sem(pfx + "qx1")]

    outer = contextlib.ExitStack()
    with outer:
        eyebf = outer.enter_context(nc.sbuf_tensor(pfx + "eyebf", [128, 128], BF16))
        bts = outer.enter_context(nc.sbuf_tensor(pfx + "bts", [128, 4 * WIN], BF16))
        bcs = outer.enter_context(nc.sbuf_tensor(pfx + "bcs", [128, JC], F32))
        bqs = outer.enter_context(nc.sbuf_tensor(pfx + "bqs", [128, JC], F32))
        bqss = outer.enter_context(nc.sbuf_tensor(pfx + "bqss", [128, JC], F32))
        bks = outer.enter_context(nc.sbuf_tensor(pfx + "bks", [128, JC], F32))
        bvs = outer.enter_context(nc.sbuf_tensor(pfx + "bvs", [128, JC], F32))
        bdbf = outer.enter_context(nc.sbuf_tensor(pfx + "bdbf", [1, H], BF16))
        ones1 = outer.enter_context(nc.sbuf_tensor(pfx + "ones1", [128, 128], BF16))
        ceps = outer.enter_context(nc.sbuf_tensor(pfx + "ceps", [128, 1], F32))
        rmax = outer.enter_context(nc.sbuf_tensor(pfx + "rmax", [128, 8], F32))
        rsum = outer.enter_context(nc.sbuf_tensor(pfx + "rsum", [128, 8], F32))
        rinv = outer.enter_context(nc.sbuf_tensor(pfx + "rinv", [128, 8], F32))
        diagR = outer.enter_context(nc.sbuf_tensor(pfx + "diagR", [128, 8 * 128], BF16))
        pbuf = outer.enter_context(nc.sbuf_tensor(pfx + "pbuf", [128, 8 * WIN], BF16))
        vpr = outer.enter_context(nc.sbuf_tensor(pfx + "vpr", [128, 2 * WIN], BF16))
        st_xc = contextlib.ExitStack()
        xcT = st_xc.enter_context(nc.sbuf_tensor(pfx + "xcT", [128, JC * TOK], BF16))
        st_mid = contextlib.ExitStack()
        outer.callback(st_mid.close)

        with contextlib.ExitStack() as st1:
            hT = st1.enter_context(nc.sbuf_tensor(pfx + "hT", [128, KT * TOK], BF16, side="right"))
            stats = st1.enter_context(nc.sbuf_tensor(pfx + "stats", [128, 16], F32, side="right"))
            dump = st1.enter_context(nc.sbuf_tensor(pfx + "dump", [128, H], BF16, side="right"))
            wc_b = st1.enter_context(nc.sbuf_tensor(pfx + "wc_b", [128, KT * HC], BF16, side="right"))
            h_tm = st1.enter_context(nc.sbuf_tensor(pfx + "h_tm", [128, 2 * H], BF16, side="right"))
            xld = st1.enter_context(nc.sbuf_tensor(pfx + "xld", [128, 2 * H], F32, side="right"))
            psA = st1.enter_context(nc.psum_tensor([128, 512], F32))
            psB = st1.enter_context(nc.psum_tensor([128, 512], F32))
            psT0 = st1.enter_context(nc.psum_tensor([128, 512], F32))
            psT1 = st1.enter_context(nc.psum_tensor([128, 512], F32))
            TB = [psT0, psT1]

            M.begin()
            M.op("gpsimd", lambda g: g.dma_start(eyebf[:], EYE), inc=qm, dma=True)
            for nm, buf in (("bc", bcs), ("bq", bqs), ("bk", bks), ("bv", bvs)):
                M.op("sync", lambda s, b=buf, nm=nm: s.dma_start(
                    b[:], WAP[nm].rearrange("(j p) -> p j", p=128),
                    allow_slow_non_contiguous=True), inc=qm, dma=True)
            M.op("gpsimd", lambda g: g.dma_start(
                bdbf[:], WAP["bd"].rearrange("(o h) -> o h", o=1)), inc=qm, dma=True)
            M.op("gpsimd", lambda g: g.dma_start(
                bts[:].rearrange("p (q w) -> p q w", q=4), BT.rearrange("q p w -> p q w")),
                inc=qm, dma=True)
            ev_misc = (qm, M.cnt[qm])
            ev_wc = M.op("gpsimd", lambda g: g.dma_start(
                wc_b[:].rearrange("p (k m) -> p k m", k=KT),
                WAP["Wc"].rearrange("(k p) m -> p k m", p=128)), inc=qwc, dma=True)
            ev_eyebf = ev_misc
            ev_ceps = M.op("vector", lambda v_: v_.memset(ceps[:], EPS), inc=v)
            M.op("vector", lambda v_: v_.memset(ones1[0:1, :], 1.0), inc=v)
            ev_bqss = M.op("vector", lambda v_: v_.tensor_scalar_mul(bqss[:], bqs[:], QSCALE),
                           after=[ev_misc], inc=v)

            trk = {"st_free": [None, None], "dump_free": None, "htm_free": [None, None],
                   "tb_free": [None, None], "hT_war": None}
            xld_free = [None, None]
            hT_ev = {}
            for tt in range(16):
                si = tt % 2
                ev_x = M.op("sync", lambda s, tt=tt, si=si: s.dma_start(
                    xld[:, si * H:(si + 1) * H], SRC[tt * 128:(tt + 1) * 128, :]),
                    after=[xld_free[si]], inc=qx[si], dma=True)
                evs = emit_ln_tile(M, (t, v, sc), xld[:, si * H:(si + 1) * H], tt, si,
                                   stats, dump, ceps, h_tm, hT, TOK,
                                   eyebf, TB, trk, ev_x, ev_eyebf, ev_ceps)
                xld_free[si] = trk["st_free"][si]
                for g2, e in enumerate(evs):
                    hT_ev[(tt, g2)] = e

            ps_free = {0: None, 1: None}
            for jc in range(JC):
                for tc in range(4):
                    pi = (jc * 4 + tc) % 2
                    ps = [psA, psB][pi]
                    deps = [hT_ev[(tt2, g2)] for tt2 in range(tc * 4, tc * 4 + 4) for g2 in range(4)]
                    mm = None
                    for k in range(KT):
                        mm = M.op("tensor", lambda t_, ps=ps, k=k, jc=jc, tc=tc:
                                  t_.matmul(ps[:], wc_b[:, k * HC + jc * 128: k * HC + (jc + 1) * 128],
                                            hT[:, k * TOK + tc * 512: k * TOK + (tc + 1) * 512],
                                            start=(k == 0), stop=(k == KT - 1)),
                                  after=(deps + [ev_wc, ps_free[pi]]) if k == 0 else (),
                                  inc=t if k == KT - 1 else None)
                    ev = M.op("scalar", lambda s, ps=ps, jc=jc, tc=tc:
                              s.activation(xcT[:, jc * TOK + tc * 512: jc * TOK + (tc + 1) * 512],
                                           ps[:], AF.Identity, bias=bcs[:, jc:jc + 1], scale=1.0),
                              after=[mm, ev_misc], inc=sc)
                    ps_free[pi] = ev
            M.end()

        qT = st_mid.enter_context(nc.sbuf_tensor(pfx + "qT", [128, JC * TOK], BF16, side="right"))
        kTt = st_mid.enter_context(nc.sbuf_tensor(pfx + "kTt", [128, JC * TOK], BF16, side="right"))
        vT = st_mid.enter_context(nc.sbuf_tensor(pfx + "vT", [128, JC * TOK], BF16, side="right"))
        with contextlib.ExitStack() as st2:
            wq_b = st2.enter_context(nc.sbuf_tensor(pfx + "wq_b", [128, JC * HC], BF16, side="right"))
            wk_b = st2.enter_context(nc.sbuf_tensor(pfx + "wk_b", [128, JC * HC], BF16, side="right"))
            wv_b = st2.enter_context(nc.sbuf_tensor(pfx + "wv_b", [128, JC * HC], BF16, side="right"))
            psA = st2.enter_context(nc.psum_tensor([128, 512], F32))
            psB = st2.enter_context(nc.psum_tensor([128, 512], F32))

            M.begin()
            ev_wq = M.op("gpsimd", lambda g: g.dma_start(
                wq_b[:].rearrange("p (k m) -> p k m", k=JC),
                WAP["Wq"].rearrange("(k p) m -> p k m", p=128)), inc=qwq, dma=True)
            ev_wk = M.op("gpsimd", lambda g: g.dma_start(
                wk_b[:].rearrange("p (k m) -> p k m", k=JC),
                WAP["Wk"].rearrange("(k p) m -> p k m", p=128)), inc=qwk, dma=True)
            ev_wv = M.op("gpsimd", lambda g: g.dma_start(
                wv_b[:].rearrange("p (k m) -> p k m", k=JC),
                WAP["Wv"].rearrange("(k p) m -> p k m", p=128)), inc=qwv, dma=True)
            ps_free = {0: None, 1: None}
            ev_q, ev_k, ev_v = {}, {}, {}
            for name, wsb, ev_w, dst, evd, bvec, scl in (
                ("q", wq_b, ev_wq, qT, ev_q, bqss, QSCALE),
                ("k", wk_b, ev_wk, kTt, ev_k, bks, 1.0),
                ("v", wv_b, ev_wv, vT, ev_v, bvs, 1.0),
            ):
                for jc in range(JC):
                    for tc in range(4):
                        pi = (jc * 4 + tc) % 2
                        ps = [psA, psB][pi]
                        mm = None
                        for ik in range(JC):
                            mm = M.op("tensor", lambda t_, ps=ps, ik=ik, jc=jc, tc=tc, w=wsb:
                                      t_.matmul(ps[:], w[:, ik * HC + jc * 128: ik * HC + (jc + 1) * 128],
                                                xcT[:, ik * TOK + tc * 512: ik * TOK + (tc + 1) * 512],
                                                start=(ik == 0), stop=(ik == JC - 1)),
                                      after=[ev_w, ps_free[pi]] if ik == 0 else (),
                                      inc=t if ik == JC - 1 else None)
                        ev = M.op("scalar", lambda s, ps=ps, jc=jc, tc=tc, d=dst, b=bvec, scl=scl:
                                  s.activation(d[:, jc * TOK + tc * 512: jc * TOK + (tc + 1) * 512],
                                               ps[:], AF.Identity, bias=b[:, jc:jc + 1], scale=scl),
                                  after=[mm], inc=sc)
                        evd[(jc, tc)] = ev
                        ps_free[pi] = ev
            M.end()
        st_xc.close()

        with contextlib.ExitStack() as st3:
            ctxT = st3.enter_context(nc.sbuf_tensor(pfx + "ctxT", [128, JC * TOK], BF16, side="right"))
            wd_b = st3.enter_context(nc.sbuf_tensor(pfx + "wd_b", [128, JC * H], BF16, side="right"))
            xres = st3.enter_context(nc.sbuf_tensor(pfx + "xres", [128, 2 * H], F32, side="right"))
            pT = st3.enter_context(nc.sbuf_tensor(pfx + "pT", [128, 2 * 4 * WIN], BF16, side="right"))
            psS0 = st3.enter_context(nc.psum_tensor([128, 512], F32))
            psS1 = st3.enter_context(nc.psum_tensor([128, 512], F32))
            psS2 = st3.enter_context(nc.psum_tensor([128, 512], F32))
            psT1 = st3.enter_context(nc.psum_tensor([128, 512], F32))
            psT2 = st3.enter_context(nc.psum_tensor([128, 512], F32))
            psC = st3.enter_context(nc.psum_tensor([64, 512], F32))
            psD = st3.enter_context(nc.psum_tensor([64, 512], F32))
            psB = st3.enter_context(nc.psum_tensor([128, 512], F32))
            SB = [psS0, psS1, psS2]

            M.begin()
            ev_wd = M.op("gpsimd", lambda g: g.dma_start(
                wd_b[:].rearrange("p (k m) -> p k m", k=JC),
                WAP["Wd"].rearrange("(k p) m -> p k m", p=128)), inc=qwd, dma=True)

            ps_free = {0: None, 1: None, 2: None}
            vp_free = None
            pt_free = {0: None, 1: None}
            pTs_free = {0: None, 1: None}
            pbuf_free = {0: None, 1: None}
            ctx_free = {0: None, 1: None}
            vpr_free = {0: None, 1: None}
            stat_free = {0: None, 1: None}
            ctx_evs = {}
            sq = 0
            it = 0
            for w in range(NWIN):
                for jc in range(JC):
                    vslot = (w * JC + jc) % 2
                    tev = None
                    for kc in range(4):
                        tev = M.op("tensor", lambda t_, kc=kc, jc=jc, w=w:
                                   t_.matmul(psB[:, kc * 128:(kc + 1) * 128],
                                             vT[:, jc * TOK + w * WIN + kc * 128: jc * TOK + w * WIN + (kc + 1) * 128],
                                             eyebf[:], start=True, stop=True),
                                   after=[ev_v[(jc, w)], vp_free] if kc == 0 else (),
                                   inc=t if kc == 3 else None)
                    ev_vpr = M.op("vector", lambda v_, s2=vslot: v_.tensor_copy(
                        vpr[:, s2 * WIN:(s2 + 1) * WIN], psB[:]),
                        after=[tev, vpr_free[vslot]], inc=v)
                    vp_free = ev_vpr
                    last_cme = None
                    for hh in range(2):
                        hb = hh * 64
                        islot = it % 2
                        tr_evs = []
                        for qc in range(4):
                            sslot = sq % 3
                            sq += 1
                            psSc = SB[sslot]
                            m1 = M.op("tensor", lambda t_, ps=psSc, jc=jc, hb=hb, qc=qc, w=w:
                                      t_.matmul(ps[:],
                                                qT[hb:hb + 64, jc * TOK + w * WIN + qc * 128: jc * TOK + w * WIN + (qc + 1) * 128],
                                                kTt[hb:hb + 64, jc * TOK + w * WIN: jc * TOK + (w + 1) * WIN],
                                                start=True, stop=False),
                                      after=[ev_q[(jc, w)], ev_k[(jc, w)], ps_free[sslot]])
                            m2 = M.op("tensor", lambda t_, ps=psSc, qc=qc:
                                      t_.matmul(ps[:], eyebf[:], bts[:, qc * WIN:(qc + 1) * WIN],
                                                start=False, stop=True),
                                      after=[ev_misc], inc=t)
                            si4 = islot * 4 + qc
                            ev_max = M.op("vector", lambda v_, ps=psSc, i=si4:
                                          v_.reduce_max(rmax[:, i:i + 1], ps[:], axis=AX, negate=True),
                                          after=[m2, stat_free[islot]], inc=v)
                            ev_exp = M.op("scalar", lambda s, ps=psSc, i=si4, sl=islot, qc=qc:
                                          s.activation(pbuf[:, (sl * 4 + qc) * WIN:(sl * 4 + qc + 1) * WIN],
                                                       ps[:], AF.Exp,
                                                       bias=rmax[:, i:i + 1], scale=1.0,
                                                       accum_out=rsum[:, i:i + 1]),
                                          after=[ev_max, pbuf_free[islot]], inc=sc)
                            ps_free[sslot] = ev_exp
                            ev_rinv = M.op("vector", lambda v_, i=si4:
                                           v_.reciprocal(rinv[:, i:i + 1], rsum[:, i:i + 1]),
                                           after=[ev_exp], inc=v)
                            ev_diag = M.op("vector", lambda v_, i=si4:
                                           v_.tensor_scalar(diagR[:, i * 128:(i + 1) * 128], eyebf[:],
                                                            rinv[:, i:i + 1], None, ALU.mult),
                                           after=[ev_rinv, ev_misc], inc=v)
                            psT = [psT1, psT2][qc % 2]
                            tpe = None
                            for kc in range(4):
                                tpe = M.op("tensor", lambda t_, ps=psT, kc=kc, i=si4:
                                           t_.matmul(ps[:, kc * 128:(kc + 1) * 128],
                                                     pbuf[:, i * WIN + kc * 128: i * WIN + (kc + 1) * 128],
                                                     diagR[:, i * 128:(i + 1) * 128], start=True, stop=True),
                                           after=[ev_exp, ev_diag, pt_free[qc % 2]] if kc == 0 else (),
                                           inc=t if kc == 3 else None)
                            pT3 = pT[:].rearrange("p (sl kc n) -> p (sl kc) n", sl=2, kc=4)
                            ev_pT = M.op("scalar", lambda s, ps=psT, sl=islot, qc=qc, pT3=pT3:
                                         s.activation(pT3[:, sl * 4: sl * 4 + 4, qc * 128:(qc + 1) * 128],
                                                      ps[:].rearrange("p (kc n) -> p kc n", kc=4),
                                                      AF.Identity),
                                         after=[tpe, pTs_free[islot] if qc == 0 else None], inc=sc)
                            pt_free[qc % 2] = ev_pT
                            tr_evs.append(ev_pT)
                        psCx = [psC, psD][islot]
                        cme = None
                        for kc in range(4):
                            cme = M.op("tensor", lambda t_, ps=psCx, kc=kc, sl=islot, s2=vslot, hb=hb:
                                       t_.matmul(ps[:],
                                                 vpr[:, s2 * WIN + kc * 128 + hb: s2 * WIN + kc * 128 + hb + 64],
                                                 pT[:, (sl * 4 + kc) * WIN: (sl * 4 + kc) * WIN + 512],
                                                 start=(kc == 0), stop=(kc == 3)),
                                       after=(tr_evs + [ev_vpr, ctx_free[islot]]) if kc == 0 else (),
                                       inc=t if kc == 3 else None)
                        pTs_free[islot] = cme
                        last_cme = cme
                        ev_ctx = M.op("vector", lambda v_, ps=psCx, jc=jc, hb=hb, w=w:
                                      v_.tensor_copy(
                                          ctxT[hb:hb + 64, jc * TOK + w * WIN: jc * TOK + (w + 1) * WIN],
                                          ps[:]),
                                      after=[cme], inc=v)
                        ctx_free[islot] = ev_ctx
                        stat_free[islot] = ev_ctx
                        pbuf_free[islot] = last_cme
                        for qc in range(4):
                            ctx_evs[(jc, hh, w, qc)] = ev_ctx
                        it += 1
                    vpr_free[vslot] = last_cme

            xres_free = {0: None, 1: None}
            out_evs = []
            for tt in range(16):
                w, qc = tt // 4, tt % 4
                xs = tt % 2
                ev_x = M.op("sync", lambda s, tt=tt, xs=xs: s.dma_start(
                    xres[:, xs * H:(xs + 1) * H], SRC[tt * 128:(tt + 1) * 128, :]),
                    after=[xres_free[xs]], inc=qx[xs], dma=True)
                hc_evs = []
                for hc in range(4):
                    pi = hc % 2
                    ps = [psS0, psS1][pi]
                    deps = [ctx_evs[(jc2, hh2, w, qc)] for jc2 in range(JC) for hh2 in range(2)]
                    mm = None
                    for jc2 in range(JC):
                        mm = M.op("tensor", lambda t_, ps=ps, jc2=jc2, w=w, qc=qc, hc=hc:
                                  t_.matmul(ps[:],
                                            ctxT[:, jc2 * TOK + w * WIN + qc * 128: jc2 * TOK + w * WIN + (qc + 1) * 128],
                                            wd_b[:, jc2 * H + hc * 512: jc2 * H + (hc + 1) * 512],
                                            start=(jc2 == 0), stop=False),
                                  after=(deps + [ev_wd, ps_free[pi]]) if jc2 == 0 else ())
                    mm = M.op("tensor", lambda t_, ps=ps, hc=hc:
                              t_.matmul(ps[:], ones1[0:1, :], bdbf[0:1, hc * 512:(hc + 1) * 512],
                                        start=False, stop=True),
                              after=[ev_misc], inc=t)
                    ev_o = M.op("vector", lambda v_, ps=ps, xs=xs, hc=hc: v_.tensor_add(
                        xres[:, xs * H + hc * 512: xs * H + (hc + 1) * 512],
                        xres[:, xs * H + hc * 512: xs * H + (hc + 1) * 512], ps[:]),
                        after=[mm, ev_x], inc=v)
                    ps_free[pi] = ev_o
                    hc_evs.append(ev_o)
                ev_st = M.op("sync", lambda s, tt=tt, xs=xs: s.dma_start(
                    DST[tt * 128:(tt + 1) * 128, :], xres[:, xs * H:(xs + 1) * H]),
                    after=hc_evs, inc=qo, dma=True)
                xres_free[xs] = ev_st
                out_evs.append(ev_st)
            M.op("sync", lambda s: s.nop(), after=out_evs)
            M.end()


def build_program():
    nc = bass.Bass("TRN2", target_bir_lowering=False)
    X = nc.dram_tensor("x", [TOK, H], F32, kind="ExternalInput").ap()
    Y = nc.dram_tensor("y", [TOK, H], F32, kind="ExternalOutput").ap()
    X1 = nc.dram_tensor("x1", [TOK, H], F32, kind="Internal").ap()
    X2 = nc.dram_tensor("x2", [TOK, H], F32, kind="Internal").ap()
    EYE = nc.dram_tensor("eye", [128, 128], F32, kind="ExternalInput").ap()
    GWT = nc.dram_tensor("gwT", [H, 8], F32, kind="ExternalInput").ap()
    WUT = nc.dram_tensor("wuT", [8, H, F], F32, kind="ExternalInput").ap()
    WGT = nc.dram_tensor("wgT", [8, H, F], F32, kind="ExternalInput").ap()
    WDT = nc.dram_tensor("wdT", [8, F, H], F32, kind="ExternalInput").ap()
    SHU = nc.dram_tensor("shuT", [H, 2816], F32, kind="ExternalInput").ap()
    SHG = nc.dram_tensor("shgT", [H, 2816], F32, kind="ExternalInput").ap()
    SHD = nc.dram_tensor("shdT", [2816, H], F32, kind="ExternalInput").ap()
    WAP = {k: nc.dram_tensor(k, shp, F32, kind="ExternalInput").ap() for k, shp in (
        ("Wc", [H, HC]), ("Wq", [HC, HC]), ("Wk", [HC, HC]), ("Wv", [HC, HC]),
        ("Wd", [HC, H]), ("bc", [HC]), ("bq", [HC]), ("bk", [HC]), ("bv", [HC]),
        ("bd", [H]))}
    BT = nc.dram_tensor("bt", [4, 128, WIN], F32, kind="ExternalInput").ap()

    WUB = nc.dram_tensor("wub_c", [8, H, F], BF16, kind="Internal").ap()
    WGB = nc.dram_tensor("wgb_c", [8, H, F], BF16, kind="Internal").ap()
    WDB = nc.dram_tensor("wdb_c", [8, F, H], BF16, kind="Internal").ap()
    SHUB = nc.dram_tensor("shu_c", [H, 2816], BF16, kind="Internal").ap()
    SHGB = nc.dram_tensor("shg_c", [H, 2816], BF16, kind="Internal").ap()
    SHDB = nc.dram_tensor("shd_c", [2816, H], BF16, kind="Internal").ap()

    def mk_routed(e):
        return (
            lambda f0, fw, e=e: WUB[e, :, f0:f0 + fw].rearrange("(k p) f -> p k f", p=128),
            lambda f0, fw, e=e: WGB[e, :, f0:f0 + fw].rearrange("(k p) f -> p k f", p=128),
            lambda h0, hw, e=e: WDB[e, :, h0:h0 + hw].rearrange("(k p) h -> p k h", p=128),
        )

    CASTS = [(SHUB, SHU), (SHGB, SHG), (SHDB, SHD)]
    for e in range(8):
        CASTS += [(WUB[e], WUT[e]), (WGB[e], WGT[e]), (WDB[e], WDT[e])]
    EXP = {"shared": (SHUB, SHGB, SHDB), "routed": [mk_routed(e) for e in range(8)],
           "casts": CASTS}
    SHST = nc.dram_tensor("shstage", [HTOK, H], BF16, kind="Internal").ap()

    M = Mega(nc)
    for pfx in ("f1", "f2"):
        for s in ("t", "v", "sc", "gp", "qm", "qx", "qo", "qu0", "qu1", "qd0", "qd1",
                  "qs0", "qs1", "qh0", "qh1", "qg0", "qg1"):
            M.sem(pfx + s)
    for i in range(33):
        M.sem("f1qc%d" % i)
    for s in ("t", "v", "sc", "qm", "qo", "qwc", "qwq", "qwk", "qwv", "qwd", "qx0", "qx1"):
        M.sem("at" + s)

    with contextlib.ExitStack() as semstack:
        M.sems = {n: semstack.enter_context(nc.semaphore(n)) for n in M.semnames}
        ffn_sublayer(nc, M, "f1", X, X1, EXP, GWT, EYE, SHST)
        attn_sublayer(nc, M, "at", X1, X2, WAP, EYE, BT)
        ffn_sublayer(nc, M, "f2", X2, Y, EXP, GWT, EYE, SHST)
    return nc


_NC_CACHE = None


def kernel(**inputs):
    global _NC_CACHE
    f32 = np.float32
    x = np.ascontiguousarray(np.asarray(inputs["L0_attn_hidden_states"], dtype=f32)).reshape(16384, H)
    shards = [np.ascontiguousarray(x[c * TOK:(c + 1) * TOK]) for c in range(NCORES)]

    gate_w = np.asarray(inputs["L0_ffn_gate_w"], dtype=f32)
    up_w = np.asarray(inputs["L0_ffn_up_w"], dtype=f32)
    down_w = np.asarray(inputs["L0_ffn_down_w"], dtype=f32)
    com = {
        "eye": np.eye(128, dtype=f32),
        "gwT": np.ascontiguousarray(np.asarray(inputs["L0_ffn_gate_weight"], dtype=f32).T),
        "wuT": np.ascontiguousarray(np.transpose(up_w, (0, 2, 1))),
        "wgT": np.ascontiguousarray(np.transpose(gate_w, (0, 2, 1))),
        "wdT": np.ascontiguousarray(np.transpose(down_w, (0, 2, 1))),
        "shuT": np.ascontiguousarray(np.asarray(inputs["L0_ffn_sh_up_w"], dtype=f32).T),
        "shgT": np.ascontiguousarray(np.asarray(inputs["L0_ffn_sh_gate_w"], dtype=f32).T),
        "shdT": np.ascontiguousarray(np.asarray(inputs["L0_ffn_sh_down_w"], dtype=f32).T),
        "Wc": np.ascontiguousarray(np.asarray(inputs["L0_attn_Wc"], dtype=f32)),
        "Wq": np.ascontiguousarray(np.asarray(inputs["L0_attn_Wq"], dtype=f32)),
        "Wk": np.ascontiguousarray(np.asarray(inputs["L0_attn_Wk"], dtype=f32)),
        "Wv": np.ascontiguousarray(np.asarray(inputs["L0_attn_Wv"], dtype=f32)),
        "Wd": np.ascontiguousarray(np.asarray(inputs["L0_attn_Wd"], dtype=f32)),
        "bc": np.ascontiguousarray(np.asarray(inputs["L0_attn_bc"], dtype=f32)),
        "bq": np.ascontiguousarray(np.asarray(inputs["L0_attn_bq"], dtype=f32)),
        "bk": np.ascontiguousarray(np.asarray(inputs["L0_attn_bk"], dtype=f32)),
        "bv": np.ascontiguousarray(np.asarray(inputs["L0_attn_bv"], dtype=f32)),
        "bd": np.ascontiguousarray(np.asarray(inputs["L0_attn_bd"], dtype=f32)),
    }
    relb = np.asarray(inputs["L0_attn_rel_bias"], dtype=f32)
    pos = np.arange(WIN)
    rel = np.clip(pos[None, :] - pos[:, None], -128, 128) + 128
    com["bt"] = np.ascontiguousarray(relb[rel].reshape(4, 128, WIN))

    if _NC_CACHE is None:
        _NC_CACHE = build_program()
    nc = _NC_CACHE

    in_maps = [dict(com, x=s) for s in shards]
    res = run_bass_kernel_spmd(nc, in_maps, core_ids=list(range(NCORES)))
    out = np.concatenate([res.results[c]["y"] for c in range(NCORES)], axis=0)
    return out.reshape(4, 4096, H)


# version 18
# speedup vs baseline: 1.1016x; 1.1016x over previous
import contextlib
import numpy as np
import concourse.bass as bass
import concourse.mybir as mybir
from concourse.bass_core_rust import IndirectOffsetOnAxis
from concourse.bass_utils import run_bass_kernel_spmd

F32 = mybir.dt.float32
BF16 = mybir.dt.bfloat16
I32 = mybir.dt.int32
AF = mybir.ActivationFunctionType
ALU = mybir.AluOpType
AX = mybir.AxisListType.X

NCORES = 8
H = 2048
HC = 1024
KT = H // 128
JC = HC // 128
TOK = 2048
HTOK = 1024
NTT = HTOK // 128
F = 1408
FKT = F // 128
FCS = [(0, 384), (384, 384), (768, 384), (1152, 256)]
HCS = [(i * 256, 256) for i in range(8)]
WIN = 512
NWIN = TOK // WIN
QSCALE = 0.125
EPS = 1e-5

ENGS = ("tensor", "vector", "scalar", "gpsimd", "sync")


class Mega:

    def __init__(self, nc):
        self.nc = nc
        self.cnt = {}
        self.lastw = {e: {} for e in ENGS}
        self.semnames = []
        self.ops = None
        self.sems = None

    def sem(self, name):
        if name not in self.cnt:
            self.cnt[name] = 0
            self.semnames.append(name)
        return name

    def begin(self):
        assert self.ops is None
        self.ops = {e: [] for e in ENGS}

    def op(self, eng, build, after=(), inc=None, dma=False):
        waits = []
        for ev in after:
            if ev is None:
                continue
            s, v = ev
            if self.lastw[eng].get(s, 0) >= v:
                continue
            waits.append((s, v))
            self.lastw[eng][s] = v
        ev_out = None
        if inc is not None:
            self.cnt[inc] += 16 if dma else 1
            ev_out = (inc, self.cnt[inc])
        self.ops[eng].append((waits, build, inc, dma))
        return ev_out

    def end(self):
        nc = self.nc
        ops = self.ops
        sems = self.sems
        self.ops = None
        with nc.Block() as block:
            def mk(eng_name):
                def body(engobj):
                    for waits, build, inc, dma in ops[eng_name]:
                        for s, v in waits:
                            engobj.wait_ge(sems[s], v)
                        inst = build(engobj) if build is not None else None
                        if inc is not None:
                            inst.then_inc(sems[inc], 16 if dma else 1)
                return body
            for e in ENGS:
                if ops[e]:
                    getattr(block, e)(mk(e))


def emit_ln_tile(M, sems, src_ap, tt, si, stats, dump, ceps, h_tm, hT, hT_tokstride,
                 eyebf, TB, trk, ev_src, ev_eyebf, ev_ceps):
    t, v, sc = sems
    sb = si * 8
    e_sq = M.op("scalar", lambda s, r=src_ap, sb=sb:
                s.activation(dump[:], r, AF.Square, accum_out=stats[:, sb + 0:sb + 1]),
                after=[ev_src, trk["dump_free"], trk["st_free"][si]], inc=sc)
    trk["dump_free"] = e_sq
    e_sum = M.op("vector", lambda v_, r=src_ap, sb=sb:
                 v_.reduce_sum(stats[:, sb + 2:sb + 3], r, axis=AX),
                 after=[ev_src, trk["st_free"][si]], inc=v)
    e_m = M.op("vector", lambda v_, sb=sb:
               v_.tensor_scalar(stats[:, sb + 3:sb + 4], stats[:, sb + 2:sb + 3],
                                1.0 / H, None, ALU.mult), after=[e_sum], inc=v)
    e_mm2 = M.op("vector", lambda v_, sb=sb:
                 v_.tensor_scalar(stats[:, sb + 5:sb + 6], stats[:, sb + 3:sb + 4],
                                  stats[:, sb + 3:sb + 4], -1.0, ALU.mult, ALU.mult),
                 after=[e_m], inc=v)
    e_var = M.op("vector", lambda v_, sb=sb:
                 v_.scalar_tensor_tensor(stats[:, sb + 6:sb + 7], stats[:, sb + 0:sb + 1],
                                         1.0 / H, stats[:, sb + 5:sb + 6], ALU.mult, ALU.add),
                 after=[e_sq, e_mm2], inc=v)
    e_ln = M.op("scalar", lambda s, sb=sb:
                s.activation(stats[:, sb + 1:sb + 2], stats[:, sb + 6:sb + 7], AF.Ln,
                             bias=ceps[:, 0:1], scale=1.0),
                after=[e_var, ev_ceps], inc=sc)
    e_rstd = M.op("scalar", lambda s, sb=sb:
                  s.activation(stats[:, sb + 7:sb + 8], stats[:, sb + 1:sb + 2], AF.Exp,
                               bias=0.0, scale=-0.5), after=[e_ln], inc=sc)
    e_nmr = M.op("vector", lambda v_, sb=sb:
                 v_.tensor_scalar(stats[:, sb + 4:sb + 5], stats[:, sb + 3:sb + 4],
                                  stats[:, sb + 7:sb + 8], -1.0, ALU.mult, ALU.mult),
                 after=[e_rstd], inc=v)
    e_h = M.op("scalar", lambda s, r=src_ap, sb=sb, si=si:
               s.activation(h_tm[:, si * H:(si + 1) * H], r, AF.Identity,
                            bias=stats[:, sb + 4:sb + 5], scale=stats[:, sb + 7:sb + 8]),
               after=[e_nmr, trk["htm_free"][si]], inc=sc)
    trk["st_free"][si] = e_h
    evs = []
    tev = None
    hT3 = hT[:].rearrange("p (k n) -> p k n", k=KT)
    for k in range(KT):
        bi = (k // 4) % 2
        b = TB[bi]
        tev = M.op("tensor", lambda t_, k=k, si=si, b=b:
                   t_.matmul(b[:, (k % 4) * 128:((k % 4) + 1) * 128],
                             h_tm[:, si * H + k * 128: si * H + (k + 1) * 128],
                             eyebf[:], start=True, stop=True),
                   after=[e_h, ev_eyebf, trk["tb_free"][bi]] if k % 4 == 0 else (),
                   inc=t if (k % 4 == 3) else None)
        if k % 4 == 3:
            kk0 = k - 3
            eev = M.op("vector", lambda v_, kk0=kk0, tt=tt, b=b:
                       v_.tensor_copy(hT3[:, kk0:kk0 + 4, tt * 128:(tt + 1) * 128],
                                      b[:].rearrange("p (k n) -> p k n", k=4)),
                       after=[tev, trk["hT_war"]], inc=v)
            trk["tb_free"][bi] = eev
            evs.append(eev)
    trk["htm_free"][si] = tev
    return evs


CAP = 384
CCT = CAP // 128
SGRP = [(0, 512), (512, 512)]
FCS256 = [(0, 384), (384, 256), (640, 256), (896, 256), (1152, 256)]
HPAIRS = [(i * 512, 512) for i in range(4)]


def ffn_sublayer(nc, M, pfx, SRC, DST, EXP, GWT, EYE, SH):
    t, v, sc = M.sem(pfx + "t"), M.sem(pfx + "v"), M.sem(pfx + "sc")
    gp = M.sem(pfx + "gp")
    qm, qx, qo = M.sem(pfx + "qm"), M.sem(pfx + "qx"), M.sem(pfx + "qo")
    qu = [M.sem(pfx + "qu0"), M.sem(pfx + "qu1")]
    qd = [M.sem(pfx + "qd0"), M.sem(pfx + "qd1")]
    qs = [M.sem(pfx + "qs0"), M.sem(pfx + "qs1")]

    qh = [M.sem(pfx + "qh0"), M.sem(pfx + "qh1")]
    qg = [M.sem(pfx + "qg0"), M.sem(pfx + "qg1")]

    with contextlib.ExitStack() as st:
        routed = st.enter_context(nc.sbuf_tensor(pfx + "routed", [128, NTT * H], F32))
        h_tm = st.enter_context(nc.sbuf_tensor(pfx + "h_tm", [128, 2 * H], BF16))
        hGT = st.enter_context(nc.sbuf_tensor(pfx + "hGT", [128, KT * 512 + KT * 384], BF16, is_transpose_dest=True))
        act_T = st.enter_context(nc.sbuf_tensor(pfx + "act_T", [128, FKT * 512], BF16))
        wub = st.enter_context(nc.sbuf_tensor(pfx + "wub", [128, KT * 384 + KT * 256], BF16))
        wgb = st.enter_context(nc.sbuf_tensor(pfx + "wgb", [128, KT * 384 + KT * 256], BF16))
        wdb = st.enter_context(nc.sbuf_tensor(pfx + "wdb", [128, 2 * FKT * 256], BF16))
        gw_b = st.enter_context(nc.sbuf_tensor(pfx + "gw_b", [128, KT * 8], BF16))
        eyebf = st.enter_context(nc.sbuf_tensor(pfx + "eyebf", [128, 128], BF16))
        eye32 = st.enter_context(nc.sbuf_tensor(pfx + "eye32", [128, 128], F32))
        stats = st.enter_context(nc.sbuf_tensor(pfx + "stats", [128, 16], F32))
        ceps = st.enter_context(nc.sbuf_tensor(pfx + "ceps", [128, 1], F32))
        cm1 = st.enter_context(nc.sbuf_tensor(pfx + "cm1", [128, 1], F32))
        scr = st.enter_context(nc.sbuf_tensor(pfx + "scr", [128, 64], F32))
        wmat = st.enter_context(nc.sbuf_tensor(pfx + "wmat", [128, NTT * 8], F32))
        wmat_bf = st.enter_context(nc.sbuf_tensor(pfx + "wmatb", [128, NTT * 8], BF16))
        Pm1 = st.enter_context(nc.sbuf_tensor(pfx + "Pm1", [128, NTT * 8], F32))
        selm = st.enter_context(nc.sbuf_tensor(pfx + "selm", [128, NTT * 8], F32))
        CJ = st.enter_context(nc.sbuf_tensor(pfx + "CJ", [128, CAP], F32))
        grid = st.enter_context(nc.sbuf_tensor(pfx + "grid", [128, NTT * 2], BF16))
        Sel = st.enter_context(nc.sbuf_tensor(pfx + "Sel", [128, 2 * CAP], BF16))
        SelT = st.enter_context(nc.sbuf_tensor(pfx + "SelT", [128, 2 * CCT * HTOK], BF16))
        logT = SelT[:, CCT * HTOK: CCT * HTOK + 2 * HTOK].bitcast(F32)[0:8, :]
        wgi = st.enter_context(nc.sbuf_tensor(pfx + "wgi", [128, 16], F32))
        wG = st.enter_context(nc.sbuf_tensor(pfx + "wG", [128, 2 * CCT], F32))
        idxf = st.enter_context(nc.sbuf_tensor(pfx + "idxf", [128, 2 * CCT], F32))
        idxi = st.enter_context(nc.sbuf_tensor(pfx + "idxi", [128, 2 * CCT], I32))
        hG = st.enter_context(nc.sbuf_tensor(pfx + "hG", [128, 3 * H], BF16))
        e_sp = st.enter_context(nc.sbuf_tensor(pfx + "e_sp", [128, 2 * CCT * 512], BF16))
        silu_t = st.enter_context(nc.sbuf_tensor(pfx + "silu", [128, 2 * 384], F32))
        act_tm = st.enter_context(nc.sbuf_tensor(pfx + "actm", [128, 2 * 384], BF16))
        psU0 = st.enter_context(nc.psum_tensor([128, 512], F32))
        psU1 = st.enter_context(nc.psum_tensor([128, 512], F32))
        psG0 = st.enter_context(nc.psum_tensor([128, 512], F32))
        psG1 = st.enter_context(nc.psum_tensor([128, 512], F32))
        psT0 = st.enter_context(nc.psum_tensor([128, 512], F32))
        psT1 = st.enter_context(nc.psum_tensor([128, 512], F32))
        psE0 = st.enter_context(nc.psum_tensor([128, 512], F32))
        psE1 = st.enter_context(nc.psum_tensor([128, 512], F32))
        UB, GB, TB, EB = [psU0, psU1], [psG0, psG1], [psT0, psT1], [psE0, psE1]

        M.begin()
        M.op("gpsimd", lambda g: g.dma_start(eyebf[:], EYE), inc=qm, dma=True)
        M.op("gpsimd", lambda g: g.dma_start(eye32[:], EYE), inc=qm, dma=True)
        M.op("gpsimd", lambda g: g.dma_start(
            gw_b[:].rearrange("p (k e) -> p k e", k=KT),
            GWT.rearrange("(k p) e -> p k e", p=128)), inc=qm, dma=True)
        ev_misc = (qm, M.cnt[qm])
        ev_eyebf = ev_misc
        cast_ev = {}
        if pfx == "f1":
            for ci_, (dstw, srcw) in enumerate(EXP["casts"]):
                ev = M.op("gpsimd", lambda g, d=dstw, s_=srcw: g.dma_start(d, s_),
                          inc="f1qc%d" % ci_, dma=True)
                cast_ev[ci_] = ev
            M.cast_evs = dict(cast_ev)
        else:
            cast_ev = M.cast_evs
        ev_ceps = M.op("vector", lambda v_: v_.memset(ceps[:], EPS), inc=v)
        ev_cm1 = M.op("vector", lambda v_: v_.memset(cm1[:], -1.0), inc=v)
        ev_cj = M.op("gpsimd", lambda g: g.iota(CJ[:], [[1, CAP]], base=0, channel_multiplier=0,
                                                allow_small_or_imprecise_dtypes=True), inc=gp)
        gevs = []
        for tt in range(NTT):
            gevs.append(M.op("gpsimd", lambda g, tt=tt: g.iota(
                grid[:, 2 * tt:2 * tt + 1], [[1, 1]], base=0, channel_multiplier=1,
                allow_small_or_imprecise_dtypes=True), inc=gp))
            gevs.append(M.op("vector", lambda v_, tt=tt: v_.memset(
                grid[:, 2 * tt + 1:2 * tt + 2], float(tt * 128)), inc=v))
        ev_grid_g = (gp, M.cnt[gp])
        ev_grid_v = (v, M.cnt[v])

        st_free = [None, None]
        htm_free = [None, None]
        tb_free = [None, None]
        ub_free = [None, None]
        gb_free = [None, None]
        eb_free = [None, None]
        uw_slot_free = [None, None]
        dw_slot_free = [None, None]
        silu_free = [None, None]
        actm_free = [None, None]
        hgt_free = [None, None]
        hg_free = [None, None, None]
        esp_free = [None, None]
        act_T_war = [None]
        sel_war = [None]
        selT_war = [None, None]
        lgT_war = [None]
        scr_free = [None, None]
        lg_free = [None, None]
        prev_store = [None]
        stage_readers = [None]
        logT_war = [None]
        wgi_free = [None]

        for ho in range(2):
            tok0 = ho * HTOK
            ev_x = M.op("sync", lambda s, tok0=tok0: s.dma_start(
                routed[:].rearrange("p (n h) -> p n h", n=NTT),
                SRC[tok0:tok0 + HTOK, :].rearrange("(n p) h -> p n h", p=128)),
                after=[prev_store[0]], inc=qx, dma=True)

            stage_ev = {}
            for tt in range(NTT):
                si = tt % 2
                sb = si * 8
                rsl = routed[:, tt * H:(tt + 1) * H]
                e_sq = M.op("scalar", lambda s, r=rsl, sb=sb, si=si:
                            s.activation(h_tm[:, si * H:(si + 1) * H], r, AF.Square,
                                         accum_out=stats[:, sb + 0:sb + 1]),
                            after=[ev_x, htm_free[si], st_free[si]], inc=sc)
                e_sum = M.op("vector", lambda v_, r=rsl, sb=sb:
                             v_.reduce_sum(stats[:, sb + 2:sb + 3], r, axis=AX),
                             after=[ev_x, st_free[si]], inc=v)
                e_m = M.op("vector", lambda v_, sb=sb:
                           v_.tensor_scalar(stats[:, sb + 3:sb + 4], stats[:, sb + 2:sb + 3],
                                            1.0 / H, None, ALU.mult), after=[e_sum], inc=v)
                e_mm2 = M.op("vector", lambda v_, sb=sb:
                             v_.tensor_scalar(stats[:, sb + 5:sb + 6], stats[:, sb + 3:sb + 4],
                                              stats[:, sb + 3:sb + 4], -1.0, ALU.mult, ALU.mult),
                             after=[e_m], inc=v)
                e_var = M.op("vector", lambda v_, sb=sb:
                             v_.scalar_tensor_tensor(stats[:, sb + 6:sb + 7], stats[:, sb + 0:sb + 1],
                                                     1.0 / H, stats[:, sb + 5:sb + 6], ALU.mult, ALU.add),
                             after=[e_sq, e_mm2], inc=v)
                e_ln = M.op("scalar", lambda s, sb=sb:
                            s.activation(stats[:, sb + 1:sb + 2], stats[:, sb + 6:sb + 7], AF.Ln,
                                         bias=ceps[:, 0:1], scale=1.0), after=[e_var, ev_ceps], inc=sc)
                e_rstd = M.op("scalar", lambda s, sb=sb:
                              s.activation(stats[:, sb + 7:sb + 8], stats[:, sb + 1:sb + 2], AF.Exp,
                                           bias=0.0, scale=-0.5), after=[e_ln], inc=sc)
                e_nmr = M.op("vector", lambda v_, sb=sb:
                             v_.tensor_scalar(stats[:, sb + 4:sb + 5], stats[:, sb + 3:sb + 4],
                                              stats[:, sb + 7:sb + 8], -1.0, ALU.mult, ALU.mult),
                             after=[e_rstd], inc=v)
                e_h = M.op("scalar", lambda s, r=rsl, sb=sb, si=si:
                           s.activation(h_tm[:, si * H:(si + 1) * H], r, AF.Identity,
                                        bias=stats[:, sb + 4:sb + 5], scale=stats[:, sb + 7:sb + 8]),
                           after=[e_nmr], inc=sc)
                st_free[si] = e_h
                e_st = M.op("sync", lambda s, tt=tt, si=si: s.dma_start(
                    SH[tt * 128:(tt + 1) * 128, :], h_tm[:, si * H:(si + 1) * H]),
                    after=[e_h, stage_readers[0]], inc=qs[si], dma=True)
                htm_free[si] = e_st
                stage_ev[tt] = e_st
            all_staged = [stage_ev[tt] for tt in range(NTT)]

            def hgt_base_ks(slot):
                return (0, 512) if slot == 0 else (KT * 512, 384)

            def build_group(g0, width, slot):
                assert slot == 0 and width <= 512
                deps = [stage_ev[tt2] for tt2 in range(g0 // 128, (g0 + width) // 128)]
                deps += [hgt_free[slot]]
                evs = []
                for k in range(KT):
                    evs.append(M.op("sync", lambda s, k=k, g0=g0, width=width: s.dma_start(
                        hGT[:, k * 512: k * 512 + width],
                        SH[g0:g0 + width, k * 128:(k + 1) * 128], transpose=True),
                        after=deps if k == 0 else (), inc=qh[slot], dma=True))
                return evs[-1]

            def gather_issue(islot, ev_idx):
                evs = []
                for ct in range(CCT):
                    e_g = M.op("gpsimd", lambda g, ct=ct, islot=islot: g.indirect_dma_start(
                        hG[:, ct * H:(ct + 1) * H], None, SH,
                        IndirectOffsetOnAxis(ap=idxi[:, islot * CCT + ct: islot * CCT + ct + 1], axis=0)),
                        after=(all_staged + [ev_idx, hg_free[ct]]), inc=qg[ct % 2], dma=True)
                    evs.append(e_g)
                return evs

            def gather_transposes(eslot, g_evs):
                last_ev = None
                for ct in range(CCT):
                    for k4 in range(4):
                        b = TB[k4 % 2]
                        tev = None
                        for j in range(4):
                            k = k4 * 4 + j
                            tev = M.op("tensor", lambda t_, k=k, j=j, b=b, ct=ct:
                                       t_.matmul(b[:, j * 128:(j + 1) * 128],
                                                 hG[:, ct * H + k * 128: ct * H + (k + 1) * 128],
                                                 eyebf[:], start=True, stop=True),
                                       after=[g_evs[ct], ev_eyebf, tb_free[k4 % 2], hgt_free[eslot]] if j == 0 else (),
                                       inc=t if j == 3 else None)
                        base, ks = hgt_base_ks(eslot)
                        hgt3 = hGT[:, base:base + KT * ks].rearrange("p (k n) -> p k n", k=KT)
                        eev = M.op("scalar", lambda s_, k4=k4, ct=ct, b=b, hgt3=hgt3:
                                   s_.copy(hgt3[:, k4 * 4: k4 * 4 + 4, ct * 128:(ct + 1) * 128],
                                           b[:].rearrange("p (k n) -> p k n", k=4)),
                                   after=[tev], inc=sc)
                        tb_free[k4 % 2] = eev
                        last_ev = eev
                    hg_free[ct] = tev
                return last_ev

            grp_ev = {}
            ev_lgT = {}
            ug_loads = {}
            d_loads = {}
            routed_last = {}

            def sh_w(mat, fh):
                def srcf(f0, fw, mat=mat, fh=fh):
                    return mat[:, fh * F + f0: fh * F + f0 + fw].rearrange("(k p) f -> p k f", p=128)
                return srcf

            def sh_d(mat, fh):
                def srcf(h0, hw, mat=mat, fh=fh):
                    return mat[fh * F:(fh + 1) * F, h0:h0 + hw].rearrange("(k p) h -> p k h", p=128)
                return srcf

            SHU_ap, SHG_ap, SHD_ap = EXP["shared"]
            def unit(hgt_slot, width, wcol_of, store_tile0, scatter, wu_src, wg_src, wd_src,
                     hgt_ready, mid_cb=None, cast_dep=None):
                nct = width // 128

                def issue_ug_load(ci):
                    f0, fw = FCS256[ci]
                    slot = ci % 2
                    sbase = 0 if slot == 0 else KT * 384
                    wsl_u = wub[:, sbase: sbase + KT * fw]
                    wsl_g = wgb[:, sbase: sbase + KT * fw]
                    dep = [uw_slot_free[slot], cast_dep]
                    M.op("sync", lambda g, w=wsl_u, f0=f0, fw=fw: g.dma_start(
                        w.rearrange("p (k f) -> p k f", k=KT), wu_src(f0, fw)),
                        after=dep, inc=qu[slot], dma=True)
                    e_g2 = M.op("sync", lambda g, w=wsl_g, f0=f0, fw=fw: g.dma_start(
                        w.rearrange("p (k f) -> p k f", k=KT), wg_src(f0, fw)),
                        after=dep, inc=qu[slot], dma=True)
                    ug_loads[ci] = (e_g2, wsl_u, wsl_g, slot)

                issue_ug_load(0)
                issue_ug_load(1)

                def issue_d_load(hc2):
                    h0 = hc2 * 256
                    slot = hc2 % 2
                    wsl = wdb[:, slot * FKT * 256:(slot + 1) * FKT * 256]
                    e_d = M.op("sync", lambda g, w=wsl, h0=h0: g.dma_start(
                        w.rearrange("p (k h) -> p k h", k=FKT), wd_src(h0, 256)),
                        after=[dw_slot_free[slot], cast_dep], inc=qd[slot], dma=True)
                    d_loads[hc2] = (e_d, wsl, slot)

                issue_d_load(0)
                issue_d_load(1)
                mid_ret = mid_cb() if mid_cb is not None else None
                pend = [None]

                def emit_transposes(p):
                    e_act, si2, fw, f0, ct = p
                    nblk = fw // 128
                    tb_i = ct % 2
                    tev = None
                    for j in range(nblk):
                        tev = M.op("tensor", lambda t_, j=j, i=si2, b=TB[tb_i]:
                                   t_.matmul(b[:, j * 128:(j + 1) * 128],
                                             act_tm[:, i * 384 + j * 128: i * 384 + (j + 1) * 128],
                                             eyebf[:], start=True, stop=True),
                                   after=[e_act, ev_eyebf, tb_free[tb_i], act_T_war[0]] if j == 0 else (),
                                   inc=t if j == nblk - 1 else None)
                    fk0 = f0 // 128
                    aT3 = act_T[:].rearrange("p (k n) -> p k n", k=FKT)
                    eev = M.op("scalar", lambda s_, fk0=fk0, nblk=nblk, ct=ct, b=TB[tb_i]:
                               s_.copy(aT3[:, fk0:fk0 + nblk, ct * 128:(ct + 1) * 128],
                                       b[:, 0:nblk * 128].rearrange("p (k n) -> p k n", k=nblk)),
                               after=[tev], inc=sc)
                    tb_free[tb_i] = eev
                    actm_free[si2] = tev
                    return eev

                for ci, (f0, fw) in enumerate(FCS256):
                    e_ug, wsl_u, wsl_g, slot = ug_loads[ci]
                    last_mm_slot = None
                    for ct in range(nct):
                        bu, bg = UB[ct % 2], GB[ct % 2]
                        deps = [hgt_ready, e_ug, ub_free[ct % 2], gb_free[ct % 2]]
                        mm = None
                        base, ks = hgt_base_ks(hgt_slot)
                        for k in range(KT):
                            lh = hGT[:, base + k * ks + ct * 128: base + k * ks + (ct + 1) * 128]
                            M.op("tensor", lambda t_, bu=bu, lh=lh, w=wsl_u, k=k, fw=fw:
                                 t_.matmul(bu[:, 0:fw], lh, w[:, k * fw:(k + 1) * fw],
                                           start=(k == 0), stop=(k == KT - 1)),
                                 after=deps if k == 0 else ())
                            mm = M.op("tensor", lambda t_, bg=bg, lh=lh, w=wsl_g, k=k, fw=fw:
                                      t_.matmul(bg[:, 0:fw], lh, w[:, k * fw:(k + 1) * fw],
                                                start=(k == 0), stop=(k == KT - 1)),
                                      inc=t if k == KT - 1 else None)
                        last_mm_slot = mm
                        si2 = ct % 2
                        e_silu = M.op("scalar", lambda s, bg=bg, fw=fw, i=si2:
                                      s.activation(silu_t[:, i * 384: i * 384 + fw], bg[:, 0:fw], AF.Silu),
                                      after=[mm, silu_free[si2]], inc=sc)
                        gb_free[ct % 2] = e_silu
                        wc_ap = wcol_of(ct)
                        if wc_ap is None:
                            e_act = M.op("vector", lambda v_, bu=bu, fw=fw, i=si2:
                                         v_.scalar_tensor_tensor(act_tm[:, i * 384:i * 384 + fw],
                                                                 bu[:, 0:fw], 1.0,
                                                                 silu_t[:, i * 384:i * 384 + fw],
                                                                 ALU.mult, ALU.mult),
                                         after=[mm, e_silu, actm_free[si2]], inc=v)
                        else:
                            e_act = M.op("vector", lambda v_, bu=bu, fw=fw, i=si2, wc=wc_ap:
                                         v_.scalar_tensor_tensor(act_tm[:, i * 384:i * 384 + fw],
                                                                 bu[:, 0:fw], wc,
                                                                 silu_t[:, i * 384:i * 384 + fw],
                                                                 ALU.mult, ALU.mult),
                                         after=[mm, e_silu, actm_free[si2]], inc=v)
                        ub_free[ct % 2] = e_act
                        silu_free[si2] = e_act
                        if pend[0] is not None:
                            emit_transposes(pend[0])
                        pend[0] = (e_act, si2, fw, f0, ct)
                    uw_slot_free[slot] = last_mm_slot
                    if ci + 2 < len(FCS256):
                        issue_ug_load(ci + 2)
                act_T_ready = emit_transposes(pend[0])
                pend[0] = None

                last_dmm = None
                last_scatter = None
                pend_sc = [None]

                def emit_scatter(ps):
                    hp, espslot, pair_evs = ps
                    nonlocal last_scatter
                    selT_ready, sslot = scatter
                    for tt2 in range(NTT):
                        gbk = GB[tt2 % 2]
                        mm = None
                        for ct in range(nct):
                            mm = M.op("tensor", lambda t_, b=gbk, ct=ct, tt2=tt2, es=espslot, ss=sslot:
                                      t_.matmul(b[:], SelT[:, ss * CCT * HTOK + ct * HTOK + tt2 * 128:
                                                           ss * CCT * HTOK + ct * HTOK + (tt2 + 1) * 128],
                                                e_sp[:, es * CCT * 512 + ct * 512: es * CCT * 512 + (ct + 1) * 512],
                                                start=(ct == 0), stop=(ct == nct - 1)),
                                      after=(pair_evs + [selT_ready, gb_free[tt2 % 2]]) if ct == 0 else (),
                                      inc=t if ct == nct - 1 else None)
                        last_scatter = mm
                        rsl2 = routed[:, tt2 * H + hp * 512: tt2 * H + (hp + 1) * 512]
                        e_r = M.op("vector", lambda v_, b=gbk, r=rsl2:
                                   v_.scalar_tensor_tensor(r, b[:], 0.5, r, ALU.mult, ALU.add),
                                   after=[mm, routed_last.get((tt2, 2 * hp)),
                                          routed_last.get((tt2, 2 * hp + 1))], inc=v)
                        routed_last[(tt2, 2 * hp)] = e_r
                        routed_last[(tt2, 2 * hp + 1)] = e_r
                        gb_free[tt2 % 2] = e_r
                    esp_free[espslot] = last_scatter

                for hp in range(4):
                    pair_evs = []
                    espslot = hp % 2
                    for hs in range(2):
                        hc2 = hp * 2 + hs
                        e_d, wsl, slot = d_loads[hc2]
                        last_mm_slot = None
                        for ct in range(nct):
                            bi = (hc2 * nct + ct) % 2
                            bank = EB[bi]
                            mm = None
                            for fk in range(FKT):
                                la = act_T[:, fk * 512 + ct * 128: fk * 512 + (ct + 1) * 128]
                                mm = M.op("tensor", lambda t_, b=bank, la=la, w=wsl, fk=fk:
                                          t_.matmul(b[:, 0:256], la, w[:, fk * 256:(fk + 1) * 256],
                                                    start=(fk == 0), stop=(fk == FKT - 1)),
                                          after=[act_T_ready, e_d, eb_free[bi]] if fk == 0 else (),
                                          inc=t if fk == FKT - 1 else None)
                            last_mm_slot = mm
                            last_dmm = mm
                            if scatter is None:
                                rsl2 = routed[:, (store_tile0 + ct) * H + hc2 * 256:
                                              (store_tile0 + ct) * H + hc2 * 256 + 256]
                                e_r = M.op("vector", lambda v_, b=bank, r=rsl2:
                                           v_.scalar_tensor_tensor(r, b[:, 0:256], 0.5, r, ALU.mult, ALU.add),
                                           after=[mm, routed_last.get((store_tile0 + ct, hc2))], inc=v)
                                routed_last[(store_tile0 + ct, hc2)] = e_r
                                eb_free[bi] = e_r
                            else:
                                ee = M.op("vector", lambda v_, b=bank, ct=ct, hs=hs, es=espslot:
                                          v_.tensor_copy(
                                              e_sp[:, es * CCT * 512 + ct * 512 + hs * 256:
                                                   es * CCT * 512 + ct * 512 + hs * 256 + 256],
                                              b[:, 0:256]),
                                          after=[mm, esp_free[espslot] if (ct == 0 and hs == 0) else None],
                                          inc=v)
                                pair_evs.append(ee)
                                eb_free[bi] = ee
                        dw_slot_free[slot] = last_mm_slot
                        if hc2 + 2 < 8:
                            issue_d_load(hc2 + 2)
                    if scatter is not None:
                        if pend_sc[0] is not None:
                            emit_scatter(pend_sc[0])
                        pend_sc[0] = (hp, espslot, pair_evs)
                if scatter is not None and pend_sc[0] is not None:
                    emit_scatter(pend_sc[0])
                act_T_war[0] = last_dmm
                return last_dmm, last_scatter, mid_ret

            for gi, (g0, width) in enumerate(SGRP):
                bev = build_group(g0, width, 0)
                grp_ev[gi] = bev
                mm = None
                for k in range(KT):
                    mm = M.op("tensor", lambda t_, k=k, width=width:
                              t_.matmul(psE0[0:8, 0:width], gw_b[:, k * 8:(k + 1) * 8],
                                        hGT[:, k * 512: k * 512 + width],
                                        start=(k == 0), stop=(k == KT - 1)),
                              after=[bev, ev_misc, eb_free[0]] if k == 0 else (),
                              inc=t if k == KT - 1 else None)
                ev_e = M.op("vector", lambda v_, g0=g0, width=width:
                            v_.tensor_copy(logT[0:8, g0:g0 + width], psE0[0:8, 0:width]),
                            after=[mm, logT_war[0], selT_war[1]], inc=v)
                ev_lgT[gi] = ev_e
                eb_free[0] = ev_e
                for fh in range(2):
                    dmm, _, _ = unit(0, width, lambda ct: None, g0 // 128, None,
                                     sh_w(SHU_ap, fh), sh_w(SHG_ap, fh), sh_d(SHD_ap, fh), bev,
                                     cast_dep=cast_ev[2])
                    hgt_free[0] = dmm

            wmat_ev = {}
            for tt in range(NTT):
                bk = EB[tt % 2]
                m2 = M.op("tensor", lambda t_, tt=tt, b=bk:
                          t_.matmul(b[:, 0:8], logT[0:8, tt * 128:(tt + 1) * 128],
                                    eye32[0:8, 0:8], start=True, stop=True),
                          after=[ev_lgT[0], ev_lgT[1], ev_misc, eb_free[tt % 2], lg_free[tt % 2]], inc=t)
                lgT_war[0] = m2
                li = tt % 2
                sb = li * 32
                e1 = M.op("vector", lambda v_, b=bk, sb=sb:
                          v_.reduce_max(scr[:, sb + 0:sb + 1], b[:, 0:8], axis=AX, negate=True),
                          after=[m2], inc=v)
                e2 = M.op("scalar", lambda s, b=bk, sb=sb:
                          s.activation(scr[:, sb + 8:sb + 16], b[:, 0:8], AF.Exp, bias=scr[:, sb + 0:sb + 1],
                                       accum_out=scr[:, sb + 1:sb + 2]),
                          after=[e1, scr_free[li]], inc=sc)
                eb_free[li] = e2
                lg_free[li] = e2
                e3 = M.op("vector", lambda v_, sb=sb:
                          v_.reciprocal(scr[:, sb + 2:sb + 3], scr[:, sb + 1:sb + 2]), after=[e2], inc=v)
                e4 = M.op("vector", lambda v_, sb=sb:
                          v_.tensor_scalar(scr[:, sb + 8:sb + 16], scr[:, sb + 8:sb + 16],
                                           scr[:, sb + 2:sb + 3], None, ALU.mult), after=[e3], inc=v)
                e5 = M.op("vector", lambda v_, sb=sb:
                          v_.tensor_tensor(scr[:, sb + 16:sb + 20], scr[:, sb + 8:sb + 16:2],
                                           scr[:, sb + 9:sb + 16:2], ALU.max), after=[e4], inc=v)
                e6 = M.op("vector", lambda v_, sb=sb:
                          v_.reduce_max(scr[:, sb + 0:sb + 1], scr[:, sb + 16:sb + 20], axis=AX),
                          after=[e5], inc=v)
                e7 = M.op("vector", lambda v_, sb=sb:
                          v_.tensor_scalar(scr[:, sb + 20:sb + 24], scr[:, sb + 16:sb + 20],
                                           scr[:, sb + 0:sb + 1], None, ALU.is_ge), after=[e6], inc=v)
                e8 = M.op("vector", lambda v_, sb=sb:
                          v_.scalar_tensor_tensor(scr[:, sb + 3:sb + 7], scr[:, sb + 20:sb + 24],
                                                  -1e30, scr[:, sb + 16:sb + 20], ALU.mult, ALU.add),
                          after=[e7], inc=v)
                e9 = M.op("vector", lambda v_, sb=sb:
                          v_.reduce_max(scr[:, sb + 1:sb + 2], scr[:, sb + 3:sb + 7], axis=AX),
                          after=[e8], inc=v)
                e10 = M.op("vector", lambda v_, sb=sb:
                           v_.tensor_scalar(scr[:, sb + 20:sb + 24], scr[:, sb + 16:sb + 20],
                                            scr[:, sb + 1:sb + 2], None, ALU.is_ge), after=[e9], inc=v)
                e12 = M.op("vector", lambda v_, sb=sb:
                           v_.tensor_mul(scr[:, sb + 24:sb + 32:2], scr[:, sb + 8:sb + 16:2],
                                         scr[:, sb + 20:sb + 24]), after=[e10], inc=v)
                e13 = M.op("vector", lambda v_, sb=sb:
                           v_.tensor_mul(scr[:, sb + 25:sb + 32:2], scr[:, sb + 9:sb + 16:2],
                                         scr[:, sb + 20:sb + 24]), after=[e12], inc=v)
                e14 = M.op("vector", lambda v_, sb=sb:
                           v_.reduce_max(scr[:, sb + 0:sb + 1], scr[:, sb + 24:sb + 32], axis=AX),
                           after=[e13], inc=v)
                e15 = M.op("vector", lambda v_, sb=sb:
                           v_.tensor_scalar(scr[:, sb + 8:sb + 16], scr[:, sb + 24:sb + 32],
                                            scr[:, sb + 0:sb + 1], None, ALU.is_ge), after=[e14], inc=v)
                e16 = M.op("vector", lambda v_, sb=sb:
                           v_.scalar_tensor_tensor(scr[:, sb + 16:sb + 24], scr[:, sb + 8:sb + 16],
                                                   -1e30, scr[:, sb + 24:sb + 32], ALU.mult, ALU.add),
                           after=[e15], inc=v)
                e17 = M.op("vector", lambda v_, sb=sb:
                           v_.reduce_max(scr[:, sb + 1:sb + 2], scr[:, sb + 16:sb + 24], axis=AX),
                           after=[e16], inc=v)
                e18 = M.op("vector", lambda v_, sb=sb:
                           v_.tensor_scalar(scr[:, sb + 8:sb + 16], scr[:, sb + 24:sb + 32],
                                            scr[:, sb + 1:sb + 2], None, ALU.is_ge), after=[e17], inc=v)
                e19 = M.op("vector", lambda v_, tt=tt, sb=sb:
                           v_.tensor_mul(wmat[:, tt * 8:(tt + 1) * 8], scr[:, sb + 24:sb + 32],
                                         scr[:, sb + 8:sb + 16]), after=[e18], inc=v)
                wmat_ev[tt] = e19
                scr_free[li] = e19
            all_wmat = [wmat_ev[tt] for tt in range(NTT)]
            ev_wb = M.op("vector", lambda v_: v_.tensor_copy(wmat_bf[:], wmat[:]),
                         after=all_wmat + [sel_war[0]], inc=v)
            ev_sm = M.op("vector", lambda v_: v_.tensor_scalar(selm[:], wmat[:], 0.0, None, ALU.is_gt),
                         after=all_wmat, inc=v)
            wtev = None
            for tt in range(NTT):
                bk = EB[tt % 2]
                mmw = M.op("tensor", lambda t_, tt=tt, b=bk:
                           t_.matmul(b[0:8, 0:128], wmat[:, tt * 8:(tt + 1) * 8], eye32[:],
                                     start=True, stop=True),
                           after=[wmat_ev[tt], eb_free[tt % 2], logT_war[0], (qm, M.cnt[qm])], inc=t)
                wtev = M.op("vector", lambda v_, tt=tt, b=bk:
                            v_.tensor_copy(logT[0:8, tt * 128:(tt + 1) * 128], b[0:8, 0:128]),
                            after=[mmw, lgT_war[0]], inc=v)
                eb_free[tt % 2] = wtev
            STTv = SelT[:].bitcast(F32)[0:8, 0:HTOK]
            ev_st2 = M.op("vector", lambda v_: v_.tensor_scalar(STTv, logT[0:8, :], 0.0, None, ALU.is_gt),
                          after=[wtev, sel_war[0]], inc=v)
            ev_scan = M.op("vector", lambda v_: v_.tensor_tensor_scan(
                logT[0:8, :], STTv, STTv, 0.0, ALU.add, ALU.bypass),
                after=[ev_st2], inc=v)
            pm_evs = []
            for tt in range(NTT):
                bk = EB[tt % 2]
                mmp = M.op("tensor", lambda t_, tt=tt, b=bk:
                           t_.matmul(b[:, 0:8], logT[0:8, tt * 128:(tt + 1) * 128], eye32[0:8, 0:8],
                                     start=True, stop=True),
                           after=[ev_scan, eb_free[tt % 2]], inc=t)
                logT_war[0] = mmp
                pe = M.op("scalar", lambda s, tt=tt, b=bk:
                          s.activation(Pm1[:, tt * 8:(tt + 1) * 8], b[:, 0:8], AF.Identity,
                                       bias=cm1[:, 0:1], scale=1.0),
                          after=[mmp, ev_cm1], inc=sc)
                pm_evs.append(pe)
                eb_free[tt % 2] = pe

            idx_war = [None, None]

            sel_free = [None, None]

            def prep(e):
                islot = e % 2
                bkA = TB[0]
                bkB = TB[1]
                first = [True]
                st_evs = []
                for tt in range(NTT):
                    ss = tt % 2
                    se = M.op("vector", lambda v_, tt=tt, e=e, ss=ss:
                              v_.tensor_scalar(Sel[:, ss * CAP:(ss + 1) * CAP], CJ[:],
                                               Pm1[:, tt * 8 + e: tt * 8 + e + 1],
                                               selm[:, tt * 8 + e: tt * 8 + e + 1],
                                               ALU.is_equal, ALU.mult),
                              after=[pm_evs[tt], ev_sm, ev_cj, sel_free[ss]], inc=v)
                    mm = None
                    for ct in range(CCT):
                        stf = first[0]
                        first[0] = False
                        M.op("tensor", lambda t_, b=bkA, tt=tt, ct=ct, e=e, ss=ss, stf=stf:
                             t_.matmul(b[:, ct * 4:ct * 4 + 1],
                                       Sel[:, ss * CAP + ct * 128: ss * CAP + (ct + 1) * 128],
                                       wmat_bf[:, tt * 8 + e: tt * 8 + e + 1],
                                       start=stf, stop=False),
                             after=([se, ev_wb, tb_free[0]] if stf else [se]))
                        is_last = (tt == NTT - 1 and ct == CCT - 1)
                        mm = M.op("tensor", lambda t_, b=bkA, tt=tt, ct=ct, ss=ss, last=is_last:
                                  t_.matmul(b[:, ct * 4 + 1:ct * 4 + 3],
                                            Sel[:, ss * CAP + ct * 128: ss * CAP + (ct + 1) * 128],
                                            grid[:, 2 * tt:2 * tt + 2],
                                            start=False, stop=last),
                                  after=[ev_grid_g, ev_grid_v] if tt == 0 and ct == 0 else (),
                                  inc=t if is_last else None)
                    tev = None
                    for ct in range(CCT):
                        tev = M.op("tensor", lambda t_, b=bkB, tt=tt, ct=ct, ss=ss:
                                   t_.matmul(b[:, ct * 128:(ct + 1) * 128],
                                             Sel[:, ss * CAP + ct * 128: ss * CAP + (ct + 1) * 128],
                                             eyebf[:], start=True, stop=True),
                                   after=[se, tb_free[1]] if ct == 0 else (),
                                   inc=t if ct == CCT - 1 else None)
                    sel_free[ss] = tev
                    sT3 = SelT[:, islot * CCT * HTOK:(islot + 1) * CCT * HTOK].rearrange(
                        "p (c n) -> p c n", c=CCT)
                    ste = M.op("vector", lambda v_, tt=tt, b=bkB, sT3=sT3:
                               v_.tensor_copy(sT3[:, :, tt * 128:(tt + 1) * 128],
                                              b[:, 0:CCT * 128].rearrange("p (c n) -> p c n", c=CCT)),
                               after=[tev, selT_war[islot] if tt == 0 else None], inc=v)
                    tb_free[1] = ste
                    st_evs.append(ste)
                idx_mm = mm
                ev_ev = M.op("vector", lambda v_: v_.tensor_copy(wgi[:, 0:12], bkA[:, 0:12]),
                             after=[idx_mm, wgi_free[0]], inc=v)
                tb_free[0] = ev_ev
                wi_evs = []
                for ct in range(CCT):
                    e_wg = M.op("vector", lambda v_, ct=ct, islot=islot:
                                v_.tensor_copy(wG[:, islot * CCT + ct: islot * CCT + ct + 1],
                                               wgi[:, ct * 4:ct * 4 + 1]),
                                after=[ev_ev], inc=v)
                    e_ix = M.op("vector", lambda v_, ct=ct, islot=islot:
                                v_.tensor_tensor(idxf[:, islot * CCT + ct: islot * CCT + ct + 1],
                                                 wgi[:, ct * 4 + 1:ct * 4 + 2],
                                                 wgi[:, ct * 4 + 2:ct * 4 + 3], ALU.add),
                                after=[ev_ev], inc=v)
                    wi_evs += [e_wg, e_ix]
                ev_idx = M.op("vector", lambda v_, islot=islot: v_.tensor_copy(
                    idxi[:, islot * CCT:(islot + 1) * CCT], idxf[:, islot * CCT:(islot + 1) * CCT]),
                    after=wi_evs + [idx_war[islot]], inc=v)
                wgi_free[0] = ev_idx
                g_evs = gather_issue(islot, ev_idx)
                idx_war[islot] = g_evs[-1]
                return {"islot": islot, "selT_ready": st_evs[-1], "g_evs": g_evs}

            pr = prep(0)
            hr = None
            for e in range(8):
                islot = pr["islot"]
                hr = gather_transposes(islot, pr["g_evs"])
                nxt = [None]

                def mid(e=e, nxt=nxt):
                    if e + 1 < 8:
                        nxt[0] = prep(e + 1)
                    return None

                dmm, lsc, _ = unit(islot, CAP,
                                   lambda ct, islot=islot: wG[:, islot * CCT + ct: islot * CCT + ct + 1],
                                   None, (pr["selT_ready"], islot),
                                   EXP["routed"][e][0], EXP["routed"][e][1], EXP["routed"][e][2],
                                   hr, mid_cb=mid, cast_dep=cast_ev[3 + e * 3 + 2])
                hgt_free[islot] = dmm
                selT_war[islot] = lsc
                pr = nxt[0]

            stage_readers[0] = hr
            trail = [routed_last[k] for k in routed_last]
            ev_out = M.op("sync", lambda s, tok0=tok0: s.dma_start(
                DST[tok0:tok0 + HTOK, :].rearrange("(n p) h -> p n h", p=128),
                routed[:].rearrange("p (n h) -> p n h", n=NTT)),
                after=trail, inc=qo, dma=True)
            prev_store[0] = ev_out
        M.op("sync", lambda s: s.nop(), after=[prev_store[0]])
        M.end()


def attn_sublayer(nc, M, pfx, SRC, DST, WAP, EYE, BT):
    t, v, sc = M.sem(pfx + "t"), M.sem(pfx + "v"), M.sem(pfx + "sc")
    qm, qo = M.sem(pfx + "qm"), M.sem(pfx + "qo")
    qwc, qwq, qwk, qwv, qwd = (M.sem(pfx + n) for n in ("qwc", "qwq", "qwk", "qwv", "qwd"))
    qx = [M.sem(pfx + "qx0"), M.sem(pfx + "qx1")]

    outer = contextlib.ExitStack()
    with outer:
        eyebf = outer.enter_context(nc.sbuf_tensor(pfx + "eyebf", [128, 128], BF16))
        bts = outer.enter_context(nc.sbuf_tensor(pfx + "bts", [128, 4 * WIN], BF16))
        bcs = outer.enter_context(nc.sbuf_tensor(pfx + "bcs", [128, JC], F32))
        bqs = outer.enter_context(nc.sbuf_tensor(pfx + "bqs", [128, JC], F32))
        bqss = outer.enter_context(nc.sbuf_tensor(pfx + "bqss", [128, JC], F32))
        bks = outer.enter_context(nc.sbuf_tensor(pfx + "bks", [128, JC], F32))
        bvs = outer.enter_context(nc.sbuf_tensor(pfx + "bvs", [128, JC], F32))
        bdbf = outer.enter_context(nc.sbuf_tensor(pfx + "bdbf", [1, H], BF16))
        ones1 = outer.enter_context(nc.sbuf_tensor(pfx + "ones1", [128, 128], BF16))
        ceps = outer.enter_context(nc.sbuf_tensor(pfx + "ceps", [128, 1], F32))
        rmax = outer.enter_context(nc.sbuf_tensor(pfx + "rmax", [128, 8], F32))
        rsum = outer.enter_context(nc.sbuf_tensor(pfx + "rsum", [128, 8], F32))
        rinv = outer.enter_context(nc.sbuf_tensor(pfx + "rinv", [128, 8], F32))
        diagR = outer.enter_context(nc.sbuf_tensor(pfx + "diagR", [128, 8 * 128], BF16))
        pbuf = outer.enter_context(nc.sbuf_tensor(pfx + "pbuf", [128, 8 * WIN], BF16))
        vpr = outer.enter_context(nc.sbuf_tensor(pfx + "vpr", [128, 2 * WIN], BF16))
        st_xc = contextlib.ExitStack()
        xcT = st_xc.enter_context(nc.sbuf_tensor(pfx + "xcT", [128, JC * TOK], BF16))
        st_mid = contextlib.ExitStack()
        outer.callback(st_mid.close)

        with contextlib.ExitStack() as st1:
            hT = st1.enter_context(nc.sbuf_tensor(pfx + "hT", [128, KT * TOK], BF16, side="right"))
            stats = st1.enter_context(nc.sbuf_tensor(pfx + "stats", [128, 16], F32, side="right"))
            dump = st1.enter_context(nc.sbuf_tensor(pfx + "dump", [128, H], BF16, side="right"))
            wc_b = st1.enter_context(nc.sbuf_tensor(pfx + "wc_b", [128, KT * HC], BF16, side="right"))
            h_tm = st1.enter_context(nc.sbuf_tensor(pfx + "h_tm", [128, 2 * H], BF16, side="right"))
            xld = st1.enter_context(nc.sbuf_tensor(pfx + "xld", [128, 2 * H], F32, side="right"))
            psA = st1.enter_context(nc.psum_tensor([128, 512], F32))
            psB = st1.enter_context(nc.psum_tensor([128, 512], F32))
            psT0 = st1.enter_context(nc.psum_tensor([128, 512], F32))
            psT1 = st1.enter_context(nc.psum_tensor([128, 512], F32))
            TB = [psT0, psT1]

            M.begin()
            M.op("gpsimd", lambda g: g.dma_start(eyebf[:], EYE), inc=qm, dma=True)
            ev_wc = M.op("gpsimd", lambda g: g.dma_start(
                wc_b[:].rearrange("p (k m) -> p k m", k=KT),
                WAP["Wc"].rearrange("(k p) m -> p k m", p=128)), inc=qwc, dma=True)
            for nm, buf in (("bc", bcs), ("bq", bqs), ("bk", bks), ("bv", bvs)):
                M.op("gpsimd", lambda g, b=buf, nm=nm: g.dma_start(
                    b[:], WAP[nm].rearrange("(j p) -> p j", p=128),
                    allow_slow_non_contiguous=True), inc=qm, dma=True)
            M.op("gpsimd", lambda g: g.dma_start(
                bdbf[:], WAP["bd"].rearrange("(o h) -> o h", o=1)), inc=qm, dma=True)
            M.op("gpsimd", lambda g: g.dma_start(
                bts[:].rearrange("p (q w) -> p q w", q=4), BT.rearrange("q p w -> p q w")),
                inc=qm, dma=True)
            ev_misc = (qm, M.cnt[qm])
            ev_eyebf = ev_misc
            ev_ceps = M.op("vector", lambda v_: v_.memset(ceps[:], EPS), inc=v)
            M.op("vector", lambda v_: v_.memset(ones1[0:1, :], 1.0), inc=v)
            ev_bqss = M.op("vector", lambda v_: v_.tensor_scalar_mul(bqss[:], bqs[:], QSCALE),
                           after=[ev_misc], inc=v)

            trk = {"st_free": [None, None], "dump_free": None, "htm_free": [None, None],
                   "tb_free": [None, None], "hT_war": None}
            xld_free = [None, None]
            hT_ev = {}
            for tt in range(16):
                si = tt % 2
                ev_x = M.op("sync", lambda s, tt=tt, si=si: s.dma_start(
                    xld[:, si * H:(si + 1) * H], SRC[tt * 128:(tt + 1) * 128, :]),
                    after=[xld_free[si]], inc=qx[si], dma=True)
                evs = emit_ln_tile(M, (t, v, sc), xld[:, si * H:(si + 1) * H], tt, si,
                                   stats, dump, ceps, h_tm, hT, TOK,
                                   eyebf, TB, trk, ev_x, ev_eyebf, ev_ceps)
                xld_free[si] = trk["st_free"][si]
                for g2, e in enumerate(evs):
                    hT_ev[(tt, g2)] = e

            ps_free = {0: None, 1: None}
            for jc in range(JC):
                for tc in range(4):
                    pi = (jc * 4 + tc) % 2
                    ps = [psA, psB][pi]
                    deps = [hT_ev[(tt2, g2)] for tt2 in range(tc * 4, tc * 4 + 4) for g2 in range(4)]
                    mm = None
                    for k in range(KT):
                        mm = M.op("tensor", lambda t_, ps=ps, k=k, jc=jc, tc=tc:
                                  t_.matmul(ps[:], wc_b[:, k * HC + jc * 128: k * HC + (jc + 1) * 128],
                                            hT[:, k * TOK + tc * 512: k * TOK + (tc + 1) * 512],
                                            start=(k == 0), stop=(k == KT - 1)),
                                  after=(deps + [ev_wc, ps_free[pi]]) if k == 0 else (),
                                  inc=t if k == KT - 1 else None)
                    ev = M.op("scalar", lambda s, ps=ps, jc=jc, tc=tc:
                              s.activation(xcT[:, jc * TOK + tc * 512: jc * TOK + (tc + 1) * 512],
                                           ps[:], AF.Identity, bias=bcs[:, jc:jc + 1], scale=1.0),
                              after=[mm, ev_misc], inc=sc)
                    ps_free[pi] = ev
            M.end()

        qT = st_mid.enter_context(nc.sbuf_tensor(pfx + "qT", [128, JC * TOK], BF16, side="right"))
        kTt = st_mid.enter_context(nc.sbuf_tensor(pfx + "kTt", [128, JC * TOK], BF16, side="right"))
        vT = st_mid.enter_context(nc.sbuf_tensor(pfx + "vT", [128, JC * TOK], BF16, side="right"))
        with contextlib.ExitStack() as st2:
            wq_b = st2.enter_context(nc.sbuf_tensor(pfx + "wq_b", [128, JC * HC], BF16, side="right"))
            wk_b = st2.enter_context(nc.sbuf_tensor(pfx + "wk_b", [128, JC * HC], BF16, side="right"))
            wv_b = st2.enter_context(nc.sbuf_tensor(pfx + "wv_b", [128, JC * HC], BF16, side="right"))
            psA = st2.enter_context(nc.psum_tensor([128, 512], F32))
            psB = st2.enter_context(nc.psum_tensor([128, 512], F32))

            M.begin()
            ev_wq = M.op("gpsimd", lambda g: g.dma_start(
                wq_b[:].rearrange("p (k m) -> p k m", k=JC),
                WAP["Wq"].rearrange("(k p) m -> p k m", p=128)), inc=qwq, dma=True)
            ev_wk = M.op("gpsimd", lambda g: g.dma_start(
                wk_b[:].rearrange("p (k m) -> p k m", k=JC),
                WAP["Wk"].rearrange("(k p) m -> p k m", p=128)), inc=qwk, dma=True)
            ev_wv = M.op("gpsimd", lambda g: g.dma_start(
                wv_b[:].rearrange("p (k m) -> p k m", k=JC),
                WAP["Wv"].rearrange("(k p) m -> p k m", p=128)), inc=qwv, dma=True)
            ps_free = {0: None, 1: None}
            ev_q, ev_k, ev_v = {}, {}, {}
            for name, wsb, ev_w, dst, evd, bvec, scl in (
                ("q", wq_b, ev_wq, qT, ev_q, bqss, QSCALE),
                ("k", wk_b, ev_wk, kTt, ev_k, bks, 1.0),
                ("v", wv_b, ev_wv, vT, ev_v, bvs, 1.0),
            ):
                for jc in range(JC):
                    for tc in range(4):
                        pi = (jc * 4 + tc) % 2
                        ps = [psA, psB][pi]
                        mm = None
                        for ik in range(JC):
                            mm = M.op("tensor", lambda t_, ps=ps, ik=ik, jc=jc, tc=tc, w=wsb:
                                      t_.matmul(ps[:], w[:, ik * HC + jc * 128: ik * HC + (jc + 1) * 128],
                                                xcT[:, ik * TOK + tc * 512: ik * TOK + (tc + 1) * 512],
                                                start=(ik == 0), stop=(ik == JC - 1)),
                                      after=[ev_w, ps_free[pi]] if ik == 0 else (),
                                      inc=t if ik == JC - 1 else None)
                        ev = M.op("scalar", lambda s, ps=ps, jc=jc, tc=tc, d=dst, b=bvec, scl=scl:
                                  s.activation(d[:, jc * TOK + tc * 512: jc * TOK + (tc + 1) * 512],
                                               ps[:], AF.Identity, bias=b[:, jc:jc + 1], scale=scl),
                                  after=[mm], inc=sc)
                        evd[(jc, tc)] = ev
                        ps_free[pi] = ev
            M.end()
        st_xc.close()

        with contextlib.ExitStack() as st3:
            ctxT = st3.enter_context(nc.sbuf_tensor(pfx + "ctxT", [128, JC * TOK], BF16, side="right"))
            wd_b = st3.enter_context(nc.sbuf_tensor(pfx + "wd_b", [128, JC * H], BF16, side="right"))
            xres = st3.enter_context(nc.sbuf_tensor(pfx + "xres", [128, 2 * H], F32, side="right"))
            pT = st3.enter_context(nc.sbuf_tensor(pfx + "pT", [128, 2 * 4 * WIN], BF16, side="right"))
            psS0 = st3.enter_context(nc.psum_tensor([128, 512], F32))
            psS1 = st3.enter_context(nc.psum_tensor([128, 512], F32))
            psS2 = st3.enter_context(nc.psum_tensor([128, 512], F32))
            psT1 = st3.enter_context(nc.psum_tensor([128, 512], F32))
            psT2 = st3.enter_context(nc.psum_tensor([128, 512], F32))
            psC = st3.enter_context(nc.psum_tensor([64, 512], F32))
            psD = st3.enter_context(nc.psum_tensor([64, 512], F32))
            psB = st3.enter_context(nc.psum_tensor([128, 512], F32))
            SB = [psS0, psS1, psS2]

            M.begin()
            ev_wd = M.op("gpsimd", lambda g: g.dma_start(
                wd_b[:].rearrange("p (k m) -> p k m", k=JC),
                WAP["Wd"].rearrange("(k p) m -> p k m", p=128)), inc=qwd, dma=True)

            ps_free = {0: None, 1: None, 2: None}
            vp_free = None
            pt_free = {0: None, 1: None}
            pTs_free = {0: None, 1: None}
            pbuf_free = {0: None, 1: None}
            ctx_free = {0: None, 1: None}
            vpr_free = {0: None, 1: None}
            stat_free = {0: None, 1: None}
            ctx_evs = {}
            sq = 0
            it = 0
            for w in range(NWIN):
                for jc in range(JC):
                    vslot = (w * JC + jc) % 2
                    tev = None
                    for kc in range(4):
                        tev = M.op("tensor", lambda t_, kc=kc, jc=jc, w=w:
                                   t_.matmul(psB[:, kc * 128:(kc + 1) * 128],
                                             vT[:, jc * TOK + w * WIN + kc * 128: jc * TOK + w * WIN + (kc + 1) * 128],
                                             eyebf[:], start=True, stop=True),
                                   after=[ev_v[(jc, w)], vp_free] if kc == 0 else (),
                                   inc=t if kc == 3 else None)
                    ev_vpr = M.op("vector", lambda v_, s2=vslot: v_.tensor_copy(
                        vpr[:, s2 * WIN:(s2 + 1) * WIN], psB[:]),
                        after=[tev, vpr_free[vslot]], inc=v)
                    vp_free = ev_vpr
                    last_cme = None
                    for hh in range(2):
                        hb = hh * 64
                        islot = it % 2
                        tr_evs = []
                        for qc in range(4):
                            sslot = sq % 3
                            sq += 1
                            psSc = SB[sslot]
                            m1 = M.op("tensor", lambda t_, ps=psSc, jc=jc, hb=hb, qc=qc, w=w:
                                      t_.matmul(ps[:],
                                                qT[hb:hb + 64, jc * TOK + w * WIN + qc * 128: jc * TOK + w * WIN + (qc + 1) * 128],
                                                kTt[hb:hb + 64, jc * TOK + w * WIN: jc * TOK + (w + 1) * WIN],
                                                start=True, stop=False),
                                      after=[ev_q[(jc, w)], ev_k[(jc, w)], ps_free[sslot]])
                            m2 = M.op("tensor", lambda t_, ps=psSc, qc=qc:
                                      t_.matmul(ps[:], eyebf[:], bts[:, qc * WIN:(qc + 1) * WIN],
                                                start=False, stop=True),
                                      after=[ev_misc], inc=t)
                            si4 = islot * 4 + qc
                            ev_max = M.op("vector", lambda v_, ps=psSc, i=si4:
                                          v_.reduce_max(rmax[:, i:i + 1], ps[:], axis=AX, negate=True),
                                          after=[m2, stat_free[islot]], inc=v)
                            ev_exp = M.op("scalar", lambda s, ps=psSc, i=si4, sl=islot, qc=qc:
                                          s.activation(pbuf[:, (sl * 4 + qc) * WIN:(sl * 4 + qc + 1) * WIN],
                                                       ps[:], AF.Exp,
                                                       bias=rmax[:, i:i + 1], scale=1.0,
                                                       accum_out=rsum[:, i:i + 1]),
                                          after=[ev_max, pbuf_free[islot]], inc=sc)
                            ps_free[sslot] = ev_exp
                            ev_rinv = M.op("vector", lambda v_, i=si4:
                                           v_.reciprocal(rinv[:, i:i + 1], rsum[:, i:i + 1]),
                                           after=[ev_exp], inc=v)
                            ev_diag = M.op("vector", lambda v_, i=si4:
                                           v_.tensor_scalar(diagR[:, i * 128:(i + 1) * 128], eyebf[:],
                                                            rinv[:, i:i + 1], None, ALU.mult),
                                           after=[ev_rinv, ev_misc], inc=v)
                            psT = [psT1, psT2][qc % 2]
                            tpe = None
                            for kc in range(4):
                                tpe = M.op("tensor", lambda t_, ps=psT, kc=kc, i=si4:
                                           t_.matmul(ps[:, kc * 128:(kc + 1) * 128],
                                                     pbuf[:, i * WIN + kc * 128: i * WIN + (kc + 1) * 128],
                                                     diagR[:, i * 128:(i + 1) * 128], start=True, stop=True),
                                           after=[ev_exp, ev_diag, pt_free[qc % 2]] if kc == 0 else (),
                                           inc=t if kc == 3 else None)
                            pT3 = pT[:].rearrange("p (sl kc n) -> p (sl kc) n", sl=2, kc=4)
                            ev_pT = M.op("scalar", lambda s, ps=psT, sl=islot, qc=qc, pT3=pT3:
                                         s.activation(pT3[:, sl * 4: sl * 4 + 4, qc * 128:(qc + 1) * 128],
                                                      ps[:].rearrange("p (kc n) -> p kc n", kc=4),
                                                      AF.Identity),
                                         after=[tpe, pTs_free[islot] if qc == 0 else None], inc=sc)
                            pt_free[qc % 2] = ev_pT
                            tr_evs.append(ev_pT)
                        psCx = [psC, psD][islot]
                        cme = None
                        for kc in range(4):
                            cme = M.op("tensor", lambda t_, ps=psCx, kc=kc, sl=islot, s2=vslot, hb=hb:
                                       t_.matmul(ps[:],
                                                 vpr[:, s2 * WIN + kc * 128 + hb: s2 * WIN + kc * 128 + hb + 64],
                                                 pT[:, (sl * 4 + kc) * WIN: (sl * 4 + kc) * WIN + 512],
                                                 start=(kc == 0), stop=(kc == 3)),
                                       after=(tr_evs + [ev_vpr, ctx_free[islot]]) if kc == 0 else (),
                                       inc=t if kc == 3 else None)
                        pTs_free[islot] = cme
                        last_cme = cme
                        ev_ctx = M.op("vector", lambda v_, ps=psCx, jc=jc, hb=hb, w=w:
                                      v_.tensor_copy(
                                          ctxT[hb:hb + 64, jc * TOK + w * WIN: jc * TOK + (w + 1) * WIN],
                                          ps[:]),
                                      after=[cme], inc=v)
                        ctx_free[islot] = ev_ctx
                        stat_free[islot] = ev_ctx
                        pbuf_free[islot] = last_cme
                        for qc in range(4):
                            ctx_evs[(jc, hh, w, qc)] = ev_ctx
                        it += 1
                    vpr_free[vslot] = last_cme

            xres_free = {0: None, 1: None}
            out_evs = []
            for tt in range(16):
                w, qc = tt // 4, tt % 4
                xs = tt % 2
                ev_x = M.op("sync", lambda s, tt=tt, xs=xs: s.dma_start(
                    xres[:, xs * H:(xs + 1) * H], SRC[tt * 128:(tt + 1) * 128, :]),
                    after=[xres_free[xs]], inc=qx[xs], dma=True)
                hc_evs = []
                for hc in range(4):
                    pi = hc % 2
                    ps = [psS0, psS1][pi]
                    deps = [ctx_evs[(jc2, hh2, w, qc)] for jc2 in range(JC) for hh2 in range(2)]
                    mm = None
                    for jc2 in range(JC):
                        mm = M.op("tensor", lambda t_, ps=ps, jc2=jc2, w=w, qc=qc, hc=hc:
                                  t_.matmul(ps[:],
                                            ctxT[:, jc2 * TOK + w * WIN + qc * 128: jc2 * TOK + w * WIN + (qc + 1) * 128],
                                            wd_b[:, jc2 * H + hc * 512: jc2 * H + (hc + 1) * 512],
                                            start=(jc2 == 0), stop=False),
                                  after=(deps + [ev_wd, ps_free[pi]]) if jc2 == 0 else ())
                    mm = M.op("tensor", lambda t_, ps=ps, hc=hc:
                              t_.matmul(ps[:], ones1[0:1, :], bdbf[0:1, hc * 512:(hc + 1) * 512],
                                        start=False, stop=True),
                              after=[ev_misc], inc=t)
                    ev_o = M.op("vector", lambda v_, ps=ps, xs=xs, hc=hc: v_.tensor_add(
                        xres[:, xs * H + hc * 512: xs * H + (hc + 1) * 512],
                        xres[:, xs * H + hc * 512: xs * H + (hc + 1) * 512], ps[:]),
                        after=[mm, ev_x], inc=v)
                    ps_free[pi] = ev_o
                    hc_evs.append(ev_o)
                ev_st = M.op("sync", lambda s, tt=tt, xs=xs: s.dma_start(
                    DST[tt * 128:(tt + 1) * 128, :], xres[:, xs * H:(xs + 1) * H]),
                    after=hc_evs, inc=qo, dma=True)
                xres_free[xs] = ev_st
                out_evs.append(ev_st)
            M.op("sync", lambda s: s.nop(), after=out_evs)
            M.end()


def build_program():
    nc = bass.Bass("TRN2", target_bir_lowering=False)
    X = nc.dram_tensor("x", [TOK, H], F32, kind="ExternalInput").ap()
    Y = nc.dram_tensor("y", [TOK, H], F32, kind="ExternalOutput").ap()
    X1 = nc.dram_tensor("x1", [TOK, H], F32, kind="Internal").ap()
    X2 = nc.dram_tensor("x2", [TOK, H], F32, kind="Internal").ap()
    EYE = nc.dram_tensor("eye", [128, 128], F32, kind="ExternalInput").ap()
    GWT = nc.dram_tensor("gwT", [H, 8], F32, kind="ExternalInput").ap()
    WUT = nc.dram_tensor("wuT", [8, H, F], F32, kind="ExternalInput").ap()
    WGT = nc.dram_tensor("wgT", [8, H, F], F32, kind="ExternalInput").ap()
    WDT = nc.dram_tensor("wdT", [8, F, H], F32, kind="ExternalInput").ap()
    SHU = nc.dram_tensor("shuT", [H, 2816], F32, kind="ExternalInput").ap()
    SHG = nc.dram_tensor("shgT", [H, 2816], F32, kind="ExternalInput").ap()
    SHD = nc.dram_tensor("shdT", [2816, H], F32, kind="ExternalInput").ap()
    WAP = {k: nc.dram_tensor(k, shp, F32, kind="ExternalInput").ap() for k, shp in (
        ("Wc", [H, HC]), ("Wq", [HC, HC]), ("Wk", [HC, HC]), ("Wv", [HC, HC]),
        ("Wd", [HC, H]), ("bc", [HC]), ("bq", [HC]), ("bk", [HC]), ("bv", [HC]),
        ("bd", [H]))}
    BT = nc.dram_tensor("bt", [4, 128, WIN], F32, kind="ExternalInput").ap()

    WUB = nc.dram_tensor("wub_c", [8, H, F], BF16, kind="Internal").ap()
    WGB = nc.dram_tensor("wgb_c", [8, H, F], BF16, kind="Internal").ap()
    WDB = nc.dram_tensor("wdb_c", [8, F, H], BF16, kind="Internal").ap()
    SHUB = nc.dram_tensor("shu_c", [H, 2816], BF16, kind="Internal").ap()
    SHGB = nc.dram_tensor("shg_c", [H, 2816], BF16, kind="Internal").ap()
    SHDB = nc.dram_tensor("shd_c", [2816, H], BF16, kind="Internal").ap()

    def mk_routed(e):
        return (
            lambda f0, fw, e=e: WUB[e, :, f0:f0 + fw].rearrange("(k p) f -> p k f", p=128),
            lambda f0, fw, e=e: WGB[e, :, f0:f0 + fw].rearrange("(k p) f -> p k f", p=128),
            lambda h0, hw, e=e: WDB[e, :, h0:h0 + hw].rearrange("(k p) h -> p k h", p=128),
        )

    CASTS = [(SHUB, SHU), (SHGB, SHG), (SHDB, SHD)]
    for e in range(8):
        CASTS += [(WUB[e], WUT[e]), (WGB[e], WGT[e]), (WDB[e], WDT[e])]
    EXP = {"shared": (SHUB, SHGB, SHDB), "routed": [mk_routed(e) for e in range(8)],
           "casts": CASTS}
    SHST = nc.dram_tensor("shstage", [HTOK, H], BF16, kind="Internal").ap()

    M = Mega(nc)
    for pfx in ("f1", "f2"):
        for s in ("t", "v", "sc", "gp", "qm", "qx", "qo", "qu0", "qu1", "qd0", "qd1",
                  "qs0", "qs1", "qh0", "qh1", "qg0", "qg1"):
            M.sem(pfx + s)
    for i in range(33):
        M.sem("f1qc%d" % i)
    for s in ("t", "v", "sc", "qm", "qo", "qwc", "qwq", "qwk", "qwv", "qwd", "qx0", "qx1"):
        M.sem("at" + s)

    with contextlib.ExitStack() as semstack:
        M.sems = {n: semstack.enter_context(nc.semaphore(n)) for n in M.semnames}
        ffn_sublayer(nc, M, "f1", X, X1, EXP, GWT, EYE, SHST)
        attn_sublayer(nc, M, "at", X1, X2, WAP, EYE, BT)
        ffn_sublayer(nc, M, "f2", X2, Y, EXP, GWT, EYE, SHST)
    return nc


_NC_CACHE = None


def kernel(**inputs):
    global _NC_CACHE
    f32 = np.float32
    x = np.ascontiguousarray(np.asarray(inputs["L0_attn_hidden_states"], dtype=f32)).reshape(16384, H)
    shards = [np.ascontiguousarray(x[c * TOK:(c + 1) * TOK]) for c in range(NCORES)]

    gate_w = np.asarray(inputs["L0_ffn_gate_w"], dtype=f32)
    up_w = np.asarray(inputs["L0_ffn_up_w"], dtype=f32)
    down_w = np.asarray(inputs["L0_ffn_down_w"], dtype=f32)
    com = {
        "eye": np.eye(128, dtype=f32),
        "gwT": np.ascontiguousarray(np.asarray(inputs["L0_ffn_gate_weight"], dtype=f32).T),
        "wuT": np.ascontiguousarray(np.transpose(up_w, (0, 2, 1))),
        "wgT": np.ascontiguousarray(np.transpose(gate_w, (0, 2, 1))),
        "wdT": np.ascontiguousarray(np.transpose(down_w, (0, 2, 1))),
        "shuT": np.ascontiguousarray(np.asarray(inputs["L0_ffn_sh_up_w"], dtype=f32).T),
        "shgT": np.ascontiguousarray(np.asarray(inputs["L0_ffn_sh_gate_w"], dtype=f32).T),
        "shdT": np.ascontiguousarray(np.asarray(inputs["L0_ffn_sh_down_w"], dtype=f32).T),
        "Wc": np.ascontiguousarray(np.asarray(inputs["L0_attn_Wc"], dtype=f32)),
        "Wq": np.ascontiguousarray(np.asarray(inputs["L0_attn_Wq"], dtype=f32)),
        "Wk": np.ascontiguousarray(np.asarray(inputs["L0_attn_Wk"], dtype=f32)),
        "Wv": np.ascontiguousarray(np.asarray(inputs["L0_attn_Wv"], dtype=f32)),
        "Wd": np.ascontiguousarray(np.asarray(inputs["L0_attn_Wd"], dtype=f32)),
        "bc": np.ascontiguousarray(np.asarray(inputs["L0_attn_bc"], dtype=f32)),
        "bq": np.ascontiguousarray(np.asarray(inputs["L0_attn_bq"], dtype=f32)),
        "bk": np.ascontiguousarray(np.asarray(inputs["L0_attn_bk"], dtype=f32)),
        "bv": np.ascontiguousarray(np.asarray(inputs["L0_attn_bv"], dtype=f32)),
        "bd": np.ascontiguousarray(np.asarray(inputs["L0_attn_bd"], dtype=f32)),
    }
    relb = np.asarray(inputs["L0_attn_rel_bias"], dtype=f32)
    pos = np.arange(WIN)
    rel = np.clip(pos[None, :] - pos[:, None], -128, 128) + 128
    com["bt"] = np.ascontiguousarray(relb[rel].reshape(4, 128, WIN))

    if _NC_CACHE is None:
        _NC_CACHE = build_program()
    nc = _NC_CACHE

    in_maps = [dict(com, x=s) for s in shards]
    res = run_bass_kernel_spmd(nc, in_maps, core_ids=list(range(NCORES)))
    out = np.concatenate([res.results[c]["y"] for c in range(NCORES)], axis=0)
    return out.reshape(4, 4096, H)
